# version 68
# baseline (speedup 1.0000x reference)
import contextlib
import numpy as np
import ml_dtypes
import concourse.bass as bass
import concourse.mybir as mybir
import concourse.bass_utils as bass_utils

F32 = mybir.dt.float32
BF16 = mybir.dt.bfloat16
I32 = mybir.dt.int32
AF = mybir.ActivationFunctionType
ALU = mybir.AluOpType
AX = mybir.AxisListType

T = 2048
D = 1024
NB = 8
PH = 10.0
TPI = float(2 * np.pi)
ENGINES = ("sync", "tensor", "vector", "scalar", "gpsimd")
STAGES = ["ln1", "attna", "ln2", "ffn1", "ln3", "attnb", "ln4", "ffn2"]


class Builder:
    CHAIN = {"vector": "VC", "scalar": "SC", "gpsimd": "GC"}

    def __init__(self):
        self.events = {e: [] for e in ENGINES}
        self.cnt = {}

    def tok(self, sem):
        return (sem, self.cnt.get(sem, 0))

    def op(self, eng, fn, wait=(), sem=None, inc=0, is_dma=False):
        waits = [w for w in wait if w is not None]
        chain = self.CHAIN.get(eng) if not is_dma else None
        self.events[eng].append((fn, waits, sem, inc, chain))
        tok = None
        if chain:
            self.cnt[chain] = self.cnt.get(chain, 0) + 1
            tok = (chain, self.cnt[chain])
        if inc:
            self.cnt[sem] = self.cnt.get(sem, 0) + inc
            tok = (sem, self.cnt[sem])
        return tok

    def emit(self, block, sems):
        def runner(eng_name):
            def run(eng):
                hi = {}
                chain_pos = {}
                for fn, waits, sem, inc, chain in self.events[eng_name]:
                    if chain:
                        k = chain_pos.get(chain, 0)
                        if k > 0:
                            eng.wait_ge(sems[chain], k)
                        chain_pos[chain] = k + 1
                    for ws, wv in waits:
                        if wv > 0 and hi.get(ws, 0) < wv:
                            eng.wait_ge(sems[ws], wv)
                            hi[ws] = wv
                    ins = fn(eng)
                    if inc:
                        ins.then_inc(sems[sem], inc)
                    elif chain:
                        ins.then_inc(sems[chain], 1)
                return None
            return run
        block.sync(runner("sync"))
        block.tensor(runner("tensor"))
        block.vector(runner("vector"))
        block.scalar(runner("scalar"))
        block.gpsimd(runner("gpsimd"))


def build_program(stop_after=None):
    stop_idx = STAGES.index(stop_after) + 1 if stop_after else len(STAGES)

    nc = bass.Bass("TRN2", target_bir_lowering=False)

    xT_d = nc.dram_tensor("xT", [D, T], F32, kind="ExternalInput").ap()
    wk_d = nc.dram_tensor("wk", [D, D], F32, kind="ExternalInput").ap()
    wv_d = nc.dram_tensor("wv", [D, D], F32, kind="ExternalInput").ap()
    woa_d = nc.dram_tensor("woa", [D, D], F32, kind="ExternalInput").ap()
    wts_d = nc.dram_tensor("wts", [D, D], F32, kind="ExternalInput").ap()
    wsgn_d = nc.dram_tensor("wsgn", [128, 8192], BF16,
                            kind="Internal").ap()
    wqT_d = nc.dram_tensor("wqT", [D, D], F32, kind="ExternalInput").ap()
    wkbT_d = nc.dram_tensor("wkbT", [D, D], F32, kind="ExternalInput").ap()
    wvbT_d = nc.dram_tensor("wvbT", [D, D], F32, kind="ExternalInput").ap()
    wobT_d = nc.dram_tensor("wobT", [D, D], F32, kind="ExternalInput").ap()
    vecs_d = nc.dram_tensor("vecs", [128, 48], F32, kind="ExternalInput").ap()
    bvrow_d = nc.dram_tensor("bvrow", [1, D], F32, kind="ExternalInput").ap()
    trow_d = nc.dram_tensor("trow", [128, 1024], F32, kind="ExternalInput").ap()
    isq_d = nc.dram_tensor("isq", [128, T], BF16, kind="ExternalInput").ap()
    identb_d = nc.dram_tensor("identb", [128, 128], BF16,
                              kind="ExternalInput").ap()
    maskb_d = nc.dram_tensor("maskb", [128, 128], BF16,
                             kind="ExternalInput").ap()
    yT_d = nc.dram_tensor("yT", [D, T], F32, kind="ExternalOutput").ap()
    if stop_after is not None:
        dXT = nc.dram_tensor("dbg_XT", [128, NB * T], F32,
                             kind="ExternalOutput").ap()
        dLNT = nc.dram_tensor("dbg_LNT", [128, NB * T], F32,
                              kind="ExternalOutput").ap()
        dBIG = nc.dram_tensor("dbg_BIG", [128, NB * T], F32,
                              kind="ExternalOutput").ap()
        dPOOL = nc.dram_tensor("dbg_POOL", [128, 20480], F32,
                               kind="ExternalOutput").ap()

    with contextlib.ExitStack() as stack:
        e = stack.enter_context

        XT = e(nc.sbuf_tensor("XT", [128, NB * T], F32))
        LNT = e(nc.sbuf_tensor("LNT", [128, NB * T], BF16))
        BIG = e(nc.sbuf_tensor("BIG", [128, NB * T], BF16))
        WSA = e(nc.sbuf_tensor("WSA", [128, 3 * 1024], BF16))
        WSB = e(nc.sbuf_tensor("WSB", [128, 3 * 1024], BF16))
        POOL = e(nc.sbuf_tensor("POOLBF", [128, 20480], BF16))
        PF = e(nc.sbuf_tensor("PF32", [128, 3072], F32))
        TROW = e(nc.sbuf_tensor("TROW", [128, 1024], F32))
        ISQ = e(nc.sbuf_tensor("ISQ", [128, T], BF16))
        IDENTB = e(nc.sbuf_tensor("IDENTB", [128, 128], BF16))
        MASKB = e(nc.sbuf_tensor("MASKB", [128, 128], BF16))
        VECS = e(nc.sbuf_tensor("VECS", [128, 48], F32))
        DER = e(nc.sbuf_tensor("DER", [128, 24], F32))
        BVR = e(nc.sbuf_tensor("BVR", [1, D], BF16))
        ONESB = e(nc.sbuf_tensor("ONESB", [128, 1], BF16))
        B512 = e(nc.sbuf_tensor("B512", [128, 1], F32))
        HPI = e(nc.sbuf_tensor("HPI", [128, 1], F32))
        EPSC = e(nc.sbuf_tensor("EPSC", [128, 1], F32))
        ONESRB = e(nc.sbuf_tensor("ONESRB", [33, 128], BF16))
        INT = e(nc.sbuf_tensor("INTP", [128, 1024], I32))

        def pv(c0, n):
            return POOL[:, c0:c0 + n]
        CPH = pv(0, T); SPH = pv(T, T)
        CTH2_ = (pv(2 * T, T), pv(6 * T, T))
        STH2_ = (pv(3 * T, T), pv(9 * T, T))
        VBF = pv(4 * T, T)
        PRD = pv(5 * T, T)
        SC1 = pv(7 * T, T); SC2 = pv(8 * T, T)
        SQF = POOL[:, 0:NB * T]
        RB = pv(NB * T, T)
        MRB = pv(NB * T + T, T)
        QT2 = (pv(0, T), pv(12288, T))
        KT2 = (pv(T, T), pv(14336, T))
        VSB2 = (pv(2 * T, 2080), pv(16384, 2080))
        EPA = pv(2 * T + 2080, 1024)
        EPB = pv(2 * T + 2080 + 1024, 1024)
        EPC = pv(10240, 1024)
        EPD = pv(11264, 1024)

        U = PF[:, 0:1024]
        UF = PF[:, 1024:2048]
        FS = PF[:, 2048:3072]
        FSB = (FS, INT[:].bitcast(F32))
        REC2 = (PF[0:1, 2048:2560], PF[0:1, 0:512])
        RECB2 = (PF[0:32, 2048:2560].bitcast(BF16)[0:1, 0:512],
                 PF[0:32, 0:512].bitcast(BF16)[0:1, 0:512])
        OSCR2 = (PF[0:64, 2560:3072], PF[0:64, 1024:1536])

        PC = e(nc.psum_tensor("PC", [128, 1024], F32))
        PD = e(nc.psum_tensor("PD", [128, 1024], F32))
        POB = e(nc.psum_tensor("POB", [128, 1024], F32))
        PO0 = POB[0:65, 0:512]
        PO1 = POB[0:65, 512:1024]
        PRB = e(nc.psum_tensor("PRB", [128, 1024], F32))
        PR = PRB[:, 0:512]
        PR2 = PRB[:, 512:1024]
        UB2 = (U, POB[:], PRB[:])

        sem_names = ["T", "VC", "SC", "GC", "dx", "dc", "dw", "dwa", "dwb", "dy", "dz", "dsi", "dso", "dfa", "dfb", "dx2"]
        sems = {n: e(nc.semaphore("sem_" + n)) for n in sem_names}

        b = Builder()

        for _cb in range(4):
            b.op("sync", lambda s, _cb=_cb: s.dma_start(
                out=XT[:, _cb * T:_cb * T + T],
                in_=xT_d.rearrange("(g p) t -> p g t", p=128)[:, _cb, :]),
                sem="dx", inc=16)
        tok_xa = b.tok("dx")
        for _cb in range(4, NB):
            b.op("sync", lambda s, _cb=_cb: s.dma_start(
                out=XT[:, _cb * T:_cb * T + T],
                in_=xT_d.rearrange("(g p) t -> p g t", p=128)[:, _cb, :]),
                sem="dx2", inc=16)
        tok_x = b.tok("dx2")
        tok_x_list = [tok_xa] * 4 + [tok_x] * 4
        for dst, src in ((VECS, vecs_d), (TROW, trow_d), (ISQ, isq_d),
                         (IDENTB, identb_d), (MASKB, maskb_d)):
            b.op("sync", lambda s, dst=dst, src=src: s.dma_start(
                out=dst[:], in_=src), sem="dc", inc=16)
        b.op("gpsimd", lambda g: g.dma_start(out=BVR[:], in_=bvrow_d),
             sem="dw", inc=16, is_dma=True)
        tok_consts = b.tok("dc")
        tok_bvr = b.tok("dw")
        b.op("vector", lambda v: v.memset(ONESB[:], 1.0))
        b.op("vector", lambda v: v.memset(B512[:], 512.0))
        b.op("vector", lambda v: v.memset(HPI[:], 1.5707963267948966))
        b.op("vector", lambda v: v.memset(EPSC[:], 1e-5))
        b.op("vector", lambda v: v.memset(ONESRB[0:33, :], 1.0))
        b.op("vector", lambda v: v.tensor_single_scalar(
            DER[:, 0:8], VECS[:, 0:8], 0.125, ALU.mult), wait=[tok_consts])
        b.op("vector", lambda v: v.tensor_single_scalar(
            DER[:, 8:16], VECS[:, 40:48], 1.0 / TPI, ALU.mult))
        tok_der = b.op("vector", lambda v: v.tensor_single_scalar(
            DER[:, 16:24], DER[:, 8:16], 1024.0, ALU.mult),
            sem=None, inc=0)

        def wslice_dma(dst, src_d, col0, tokens, dsem):
            def fn(g):
                return g.dma_start(
                    out=dst.rearrange("p (g c) -> p g c", c=128),
                    in_=src_d[:, col0:col0 + 128].rearrange(
                        "(g p) c -> p g c", p=128))
            return b.op("gpsimd", fn, wait=tokens, sem=dsem, inc=16,
                        is_dma=True)

        SGA = BIG[:, 7 * T:7 * T + 1024]
        SGB = BIG[:, 7 * T + 1024:8 * T]
        sgn_tok = [None] * NB
        sgn_all_l = [None]

        _sg_prev_sign = [None]
        _sg_prev_out = [None]

        def plan_sign_chunk(_mb):
            w_in = ([_sg_prev_sign[0]] if _sg_prev_sign[0] else [])
            t_in = b.op("gpsimd", lambda g, _mb=_mb: g.dma_start(
                out=SGA.rearrange("p (g c) -> p g c", c=128),
                in_=wts_d[:, _mb * 128:_mb * 128 + 128].rearrange(
                    "(g p) c -> p g c", p=128)),
                wait=w_in, sem="dsi", inc=16, is_dma=True)
            w_s = [t_in] + ([_sg_prev_out[0]] if _sg_prev_out[0] else [])
            t_s = b.op("scalar", lambda s: s.activation(
                SGB[:], SGA[:], AF.Sign), wait=w_s, sem=None, inc=0)
            _sg_prev_sign[0] = t_s
            t_o = b.op("sync", lambda s, _mb=_mb: s.dma_start(
                out=wsgn_d[:, _mb * 1024:_mb * 1024 + 1024],
                in_=SGB[:]), wait=[t_s], sem="dso", inc=16)
            _sg_prev_out[0] = t_o
            sgn_tok[_mb] = t_o
            if _mb == NB - 1:
                sgn_all_l[0] = b.tok("dso")

        ln_state = {"pr_last": [None, None], "po_reads": [],
                    "lnt_free": None}

        def emit_ln(prev_toks, four_way=False):
            copy_toks = []
            tok_tprev = (ln_state["lnt_free"]
                         if ln_state["lnt_free"] is not None
                         else b.tok("T"))
            for cb in range(NB):
                w = ([prev_toks[cb]] if len(prev_toks) == NB
                     else list(prev_toks))
                if cb % 2 == 0:
                    tk = b.op("vector", lambda v, cb=cb: v.tensor_copy(
                        LNT[:, cb * T:(cb + 1) * T],
                        XT[:, cb * T:(cb + 1) * T]),
                        wait=w + [tok_tprev], sem=None, inc=0)
                else:
                    tk = b.op("scalar", lambda s, cb=cb: s.activation(
                        LNT[:, cb * T:(cb + 1) * T],
                        XT[:, cb * T:(cb + 1) * T], AF.Copy),
                        wait=w + [tok_tprev], sem=None, inc=0)
                copy_toks.append(tk)
            sq_toks = []
            for cb in range(NB):
                if cb % 2 == 0:
                    tk = b.op("scalar", lambda s, cb=cb: s.activation(
                        SQF[:, cb * T:(cb + 1) * T],
                        LNT[:, cb * T:(cb + 1) * T],
                        AF.Square), wait=[copy_toks[cb]], sem=None, inc=0)
                else:
                    tk = b.op("vector", lambda v, cb=cb: v.tensor_tensor(
                        SQF[:, cb * T:(cb + 1) * T],
                        LNT[:, cb * T:(cb + 1) * T],
                        LNT[:, cb * T:(cb + 1) * T], ALU.mult),
                        wait=[copy_toks[cb]], sem=None, inc=0)
                sq_toks.append(tk)
            def mk_slots(p0):
                return (PF[p0:p0 + 1, 0:512], PF[p0:p0 + 1, 512:1024],
                        PF[p0:p0 + 1, 1024:1536],
                        PF[p0:p0 + 32, 1024:1536].bitcast(BF16)[0:1, 0:512],
                        PF[p0:p0 + 32, 1536:2048].bitcast(BF16)[0:1, 0:512],
                        ONESRB[p0:p0 + 1, 0:128], EPSC[p0:p0 + 1, 0:1])

            def mk_slots_i(p0):
                return (PF[p0:p0 + 1, 2048:2560],
                        PF[p0:p0 + 1, 2560:3072],
                        INT[p0:p0 + 1, 0:512].bitcast(F32),
                        INT[p0:p0 + 32, 0:512].bitcast(BF16)[0:1, 0:512],
                        INT[p0:p0 + 32, 512:1024].bitcast(BF16)
                        [0:1, 0:512],
                        ONESRB[p0:p0 + 1, 0:128], EPSC[p0:p0 + 1, 0:1])
            if four_way:
                SLOTS = (mk_slots(0), mk_slots(32),
                         mk_slots_i(0), mk_slots_i(32))
            else:
                SLOTS = (mk_slots(0), mk_slots(32))
            ev_toks = []
            out_toks = []
            apply_pending = [None]
            PBS = (0, 32, 64, 0)
            t12 = []

            def plan_stats(s4, extra):
                pb = PBS[s4]
                def mm_x(t, s4=s4, pb=pb):
                    for cb in range(NB):
                        ins = t.matmul(PO0[pb:pb + 1, :], ONESB[:],
                                       LNT[:, cb * T + s4 * 512:
                                           cb * T + s4 * 512 + 512],
                                       start=(cb == 0), stop=(cb == NB - 1))
                    return ins
                def mm_x2(t, s4=s4, pb=pb):
                    for cb in range(NB):
                        ins = t.matmul(PO1[pb:pb + 1, :], ONESB[:],
                                       SQF[:, cb * T + s4 * 512:
                                           cb * T + s4 * 512 + 512],
                                       start=(cb == 0), stop=(cb == NB - 1))
                    return ins
                t1 = b.op("tensor", mm_x, wait=copy_toks + extra,
                          sem="T", inc=1)
                t2 = b.op("tensor", mm_x2, wait=sq_toks + extra,
                          sem="T", inc=1)
                t12.append((t1, t2))

            for s4 in range(3):
                plan_stats(s4, [])
            pr_done = [None, None]
            prev_tpr1 = [None] * len(SLOTS)
            prev_tpr2 = [None] * len(SLOTS)
            for s4 in range(4):
                par = s4 % len(SLOTS)
                pb = PBS[s4]
                m_sl, s2_sl, r_sl, rb_sl, mrb_sl, ones, eps = SLOTS[par]
                t1, t2 = t12[s4]
                tvm = b.op("scalar", lambda s, pb=pb, m_sl=m_sl:
                           s.activation(
                               m_sl, PO0[pb:pb + 1, :], AF.Copy,
                               scale=1.0 / D),
                           wait=[t1], sem=None, inc=0)
                tvs = b.op("scalar", lambda s, pb=pb, s2_sl=s2_sl:
                           s.activation(
                               s2_sl, PO1[pb:pb + 1, :], AF.Copy,
                               scale=1.0 / D),
                           wait=[t2], sem=None, inc=0)
                ln_state["po_reads"] = [tvm, tvs]
                if s4 == 0:
                    plan_stats(3, [tvm, tvs])
                b.op("vector", lambda v, m_sl=m_sl, r_sl=r_sl:
                     v.tensor_tensor(r_sl, m_sl, m_sl, ALU.mult),
                     wait=[tvm, prev_tpr1[par]])
                tvar = b.op("vector", lambda v, s2_sl=s2_sl, r_sl=r_sl:
                            v.tensor_sub(s2_sl, s2_sl, r_sl),
                            wait=[tvs], sem=None, inc=0)
                if apply_pending[0] is not None:
                    apply_pending[0]()
                    apply_pending[0] = None
                tln = b.op("scalar", lambda s, r_sl=r_sl, s2_sl=s2_sl,
                           eps=eps: s.activation(
                               r_sl, s2_sl, AF.Ln, bias=eps),
                           wait=[tvar], sem=None, inc=0)
                trr = b.op("scalar", lambda s, rb_sl=rb_sl, r_sl=r_sl:
                           s.activation(rb_sl, r_sl, AF.Exp, scale=-0.5),
                           wait=[tln], sem=None, inc=0)

                def mk_tail(s4=s4, par=par, trr=trr, m_sl=m_sl,
                            rb_sl=rb_sl, mrb_sl=mrb_sl, ones=ones):
                    def run():
                        tmr = b.op("vector", lambda v: v.tensor_tensor(
                            mrb_sl, m_sl, rb_sl, ALU.mult),
                            wait=[trr, prev_tpr2[par]], sem=None, inc=0)
                        wpr = [trr] + ([pr_done[0]] if pr_done[0] else [])
                        tpr1 = b.op("tensor", lambda t: t.matmul(
                            PR[:], ones, rb_sl, start=True, stop=True),
                            wait=wpr, sem="T", inc=1)
                        e1 = b.op("scalar", lambda s: s.activation(
                            RB[:, s4 * 512:s4 * 512 + 512], PR[:],
                            AF.Copy), wait=[tpr1], sem=None, inc=0)
                        tpr2 = b.op("tensor", lambda t: t.matmul(
                            PR2[:], ones, mrb_sl, start=True, stop=True),
                            wait=[tmr] + ([pr_done[1]] if pr_done[1]
                                          else []), sem="T", inc=1)
                        e2 = b.op("scalar", lambda s: s.activation(
                            MRB[:, s4 * 512:s4 * 512 + 512], PR2[:],
                            AF.Copy), wait=[tpr2], sem=None, inc=0)
                        prev_tpr1[par] = tpr1
                        prev_tpr2[par] = tpr2
                        pr_done[0] = e1
                        pr_done[1] = e2
                        ln_state["pr_last"][0] = e1
                        ln_state["pr_last"][1] = e2
                        ev_toks.append(e2)
                        tk = None
                        for cb in range(NB):
                            sl = slice(cb * T + s4 * 512,
                                       cb * T + s4 * 512 + 512)
                            b.op("vector", lambda v, sl=sl:
                                 v.tensor_tensor(
                                     SQF[:, sl], LNT[:, sl],
                                     RB[:, s4 * 512:s4 * 512 + 512],
                                     ALU.mult), wait=[e2])
                            tk = b.op("vector", lambda v, sl=sl:
                                      v.tensor_tensor(
                                          LNT[:, sl], SQF[:, sl],
                                          MRB[:, s4 * 512:s4 * 512 + 512],
                                          ALU.subtract), sem=None, inc=0)
                        out_toks.append(tk)
                    return run
                apply_pending[0] = mk_tail()
            apply_pending[0]()
            return out_toks

        trig_state = {"u_free": [None, None, None], "sins": None,
                      "fs_free": [None, None], "uc": 0, "fc": 0}

        def emit_trig_half(utoks, cdst, sdst, sl, pf=None, ui=0):
            fi = trig_state["fc"] % 2
            trig_state["fc"] += 1
            """u (128,1024 f32, >=0, in 'turns') -> sin into sdst[:, sl],
            cos into cdst[:, sl] (bf16). u is double-buffered (ui) so the
            producer of the next half never waits this half's cos chain.
            Robust to either truncating or round-to-nearest f32->i32
            conversion: after f = u - cvt(u), subtract 1 where f > 0.5 so
            f lands in [-0.5, 0.5]."""
            u = UB2[ui]
            fs = FSB[fi]
            b.op("vector", lambda v, u=u: v.tensor_scalar(
                UF[:], u[:], 12582912.0, 12582912.0,
                ALU.add, ALU.subtract), wait=list(utoks))
            t1 = b.op("vector", lambda v, u=u, fs=fs: v.tensor_sub(
                fs, u[:], UF[:]),
                wait=[trig_state["fs_free"][fi]], sem=None, inc=0)
            ts1 = b.op("scalar", lambda s, fs=fs: s.activation(
                sdst[:, sl], fs, AF.Sin, scale=TPI),
                wait=[t1] + ([pf] if pf else []), sem=None, inc=0)
            tabs = b.op("scalar", lambda s, u=u, fs=fs: s.activation(
                u[:], fs, AF.Abs), sem=None, inc=0)
            trig_state["fs_free"][fi] = tabs
            ts2 = b.op("scalar", lambda s, u=u: s.activation(
                cdst[:, sl], u[:], AF.Sin, scale=-TPI, bias=HPI[:, 0:1]),
                wait=([pf] if pf else []), sem=None, inc=0)
            trig_state["u_free"][ui] = ts2
            trig_state["sins"] = ts2
            return ts2

        def emit_attna(ln_toks):
            wtoks = [None] * (NB + 2)
            free_ws = [None, None]

            def plan_wdma(cb):
                if cb >= NB:
                    return
                ws = WSA if cb % 2 == 0 else WSB
                dsem = "dwa" if cb % 2 == 0 else "dwb"
                wait = [free_ws[cb % 2]] if free_ws[cb % 2] else []
                wslice_dma(ws[:, 0:1024], wk_d, cb * 128, wait, dsem)
                wtoks[cb] = wslice_dma(ws[:, 1024:2048], wv_d, cb * 128, [],
                                       dsem)

            plan_wdma(0)
            plan_wdma(1)
            big_toks = []
            prev_kr_evac = [None, None]
            prev_v_evac = [None, None]
            kr_last = [None] * NB
            trig_theta = [None] * NB
            trig_done = [None] * NB
            vev_done = [None] * NB
            pool_free = [None]
            pool_free_th = [None, None]
            vbf_free = [None]

            def plan_kr_theta(cb):
                ws = WSA if cb % 2 == 0 else WSB
                kr_evac = [None] * 4
                ui_th = (trig_state["uc"] % 3,
                         (trig_state["uc"] + 1) % 3)
                for s4 in range(4):
                    th = s4 // 2
                    def mm_kr(t, s4=s4, ws=ws):
                        for i in range(NB):
                            ins = t.matmul(
                                PC[:, (s4 % 2) * 512:(s4 % 2) * 512 + 512],
                                ws[:, i * 128:i * 128 + 128],
                                LNT[:, i * T + s4 * 512:
                                    i * T + s4 * 512 + 512],
                                start=(i == 0), stop=(i == NB - 1))
                        return ins
                    w_kr = [wtoks[cb], ln_toks[s4]] +                         [kr_evac[s4 - 2] if s4 >= 2
                         else prev_kr_evac[s4 % 2]]
                    tkr = b.op("tensor", mm_kr, wait=w_kr, sem="T", inc=1)
                    ev = b.op("scalar", lambda s, s4=s4, th=th,
                              ui_th=ui_th: s.activation(
                                  UB2[ui_th[th]][:, (s4 % 2) * 512:
                                                 (s4 % 2) * 512 + 512],
                                  PC[:, (s4 % 2) * 512:
                                     (s4 % 2) * 512 + 512],
                                  AF.Identity, scale=PH / TPI,
                                  bias=B512[:, 0:1]),
                              wait=[tkr,
                                    trig_state["u_free"][ui_th[th]]]
                              + list(ln_state["po_reads"]),
                              sem=None, inc=0)
                    kr_evac[s4] = ev
                for th in range(2):
                    emit_trig_half([kr_evac[th * 2], kr_evac[th * 2 + 1]],
                                   CTH2_[cb % 2], STH2_[cb % 2],
                                   slice(th * 1024, th * 1024 + 1024),
                                   pf=pool_free_th[cb % 2], ui=ui_th[th])
                    trig_state["uc"] += 1
                prev_kr_evac[0] = kr_evac[2]
                prev_kr_evac[1] = kr_evac[3]
                kr_last[cb] = kr_evac[3]
                trig_theta[cb] = b.tok("SC")

            def plan_v_phi(cb):
                ws = WSA if cb % 2 == 0 else WSB
                v_evac = [None] * 4
                for s4 in range(4):
                    def mm_v(t, s4=s4, ws=ws):
                        for i in range(NB):
                            ins = t.matmul(
                                PD[:, (s4 % 2) * 512:(s4 % 2) * 512 + 512],
                                ws[:, 1024 + i * 128:1024 + i * 128 + 128],
                                LNT[:, i * T + s4 * 512:i * T + s4 * 512 + 512],
                                start=(i == 0), stop=(i == NB - 1))
                        return ins
                    w_v = [wtoks[cb], ln_toks[s4]] +                         [v_evac[s4 - 2] if s4 >= 2 else prev_v_evac[s4 % 2]]
                    tv = b.op("tensor", mm_v, wait=w_v, sem="T", inc=1)
                    ev = b.op("scalar", lambda s, s4=s4, cb=cb: s.activation(
                        VBF[:, s4 * 512:s4 * 512 + 512],
                        PD[:, (s4 % 2) * 512:(s4 % 2) * 512 + 512],
                        AF.Copy), wait=[tv, vbf_free[0]], sem=None, inc=0)
                    v_evac[s4] = ev
                prev_v_evac[0] = v_evac[2]
                prev_v_evac[1] = v_evac[3]
                vev_done[cb] = v_evac[3]
                for th in range(2):
                    ui = trig_state["uc"] % 3
                    gcol = DER[:, 8 + cb:8 + cb + 1]
                    g1kcol = DER[:, 16 + cb:16 + cb + 1]
                    tphi = b.op("vector",
                                lambda v, gcol=gcol, g1kcol=g1kcol, th=th,
                                ui=ui: v.tensor_scalar(
                                    UB2[ui][:], TROW[:], gcol,
                                    (g1kcol if th == 1 else 0.0),
                                    ALU.mult, ALU.add),
                                wait=[tok_consts, tok_der, kr_last[cb],
                                      trig_state["u_free"][ui]],
                                sem=None, inc=0)
                    emit_trig_half([tphi], CPH, SPH,
                                   slice(th * 1024, th * 1024 + 1024),
                                   pf=pool_free[0], ui=ui)
                    trig_state["uc"] += 1
                trig_done[cb] = b.tok("SC")

            def plan_p2(cb):
                CTH = CTH2_[cb % 2]
                STH = STH2_[cb % 2]
                gpcol = VECS[:, 24 + cb:24 + cb + 1]
                gacol = VECS[:, 32 + cb:32 + cb + 1]
                w0 = [trig_theta[cb], vev_done[cb]]
                b.op("vector", lambda v: v.memset(PRD[:, 0:1], 0.0),
                     wait=w0)
                b.op("vector", lambda v, CTH=CTH: v.tensor_tensor(
                    PRD[:, 1:T], VBF[:, 1:T], CTH[:, 0:T - 1], ALU.mult))
                b.op("vector", lambda v: v.tensor_tensor_scan(
                    SC1[:], PRD[:], PRD[:], 0.0, ALU.add, ALU.bypass))
                b.op("vector", lambda v, CTH=CTH: v.tensor_tensor(
                    SC1[:], SC1[:], CTH[:], ALU.mult))
                b.op("vector", lambda v: v.memset(PRD[:, 0:1], 0.0))
                b.op("vector", lambda v, STH=STH: v.tensor_tensor(
                    PRD[:, 1:T], VBF[:, 1:T], STH[:, 0:T - 1], ALU.mult))
                b.op("vector", lambda v: v.tensor_tensor_scan(
                    SC2[:], PRD[:], PRD[:], 0.0, ALU.add, ALU.bypass))
                tthl = b.op("vector", lambda v, STH=STH: v.tensor_tensor(
                    SC2[:], SC2[:], STH[:], ALU.mult), sem=None, inc=0)
                pool_free_th[cb % 2] = tthl
                b.op("vector", lambda v: v.tensor_tensor(
                    SC1[:], SC1[:], SC2[:], ALU.add))
                b.op("vector", lambda v, gacol=gacol: v.tensor_single_scalar(
                    SC1[:], SC1[:], gacol, ALU.mult))
                b.op("vector", lambda v: v.tensor_tensor(
                    PRD[:], VBF[:], CPH[:], ALU.mult),
                    wait=[trig_done[cb]])
                b.op("vector", lambda v: v.tensor_tensor_scan(
                    SC2[:], PRD[:], PRD[:], 0.0, ALU.add, ALU.bypass))
                b.op("vector", lambda v: v.tensor_tensor(
                    SC2[:], SC2[:], CPH[:], ALU.mult))
                b.op("vector", lambda v: v.tensor_tensor(
                    PRD[:], VBF[:], SPH[:], ALU.mult))
                b.op("vector", lambda v: v.tensor_tensor_scan(
                    VBF[:], PRD[:], PRD[:], 0.0, ALU.add, ALU.bypass))
                tphl = b.op("vector", lambda v: v.tensor_tensor(
                    VBF[:], VBF[:], SPH[:], ALU.mult), sem=None, inc=0)
                pool_free[0] = tphl
                tvf = b.op("vector", lambda v: v.tensor_tensor(
                    SC2[:], SC2[:], VBF[:], ALU.add), sem=None, inc=0)
                vbf_free[0] = tvf
                b.op("vector", lambda v, gpcol=gpcol: v.tensor_single_scalar(
                    SC2[:], SC2[:], gpcol, ALU.mult))
                b.op("vector", lambda v: v.tensor_tensor(
                    SC1[:], SC1[:], SC2[:], ALU.add))
                tk = b.op("vector", lambda v, cb=cb: v.tensor_tensor(
                    BIG[:, cb * T:(cb + 1) * T], SC1[:], ISQ[:], ALU.mult),
                    wait=[tok_consts] + ([sgn_all_l[0]] if cb == 7
                                         else []),
                    sem=None, inc=0)
                big_toks.append(tk)

            plan_kr_theta(0)
            for cb in range(NB):
                if cb + 1 < NB:
                    plan_kr_theta(cb + 1)
                plan_v_phi(cb)
                free_ws[cb % 2] = b.tok("T")
                plan_wdma(cb + 2)
                if cb < 4:
                    plan_sign_chunk(2 * cb)
                    plan_sign_chunk(2 * cb + 1)
                plan_p2(cb)
            ln_state["lnt_free"] = b.tok("T")
            res_toks = emit_proj(big_toks, woa_d, bias_col=None)
            return res_toks

        def emit_proj(src_toks, w_dram, bias_col, out_dma=False):
            free_ws = [None, None]
            wtok = [None] * (NB + 2)

            def plan_w(mb):
                if mb >= NB:
                    return
                ws = WSA if mb % 2 == 0 else WSB
                dsem = "dwa" if mb % 2 == 0 else "dwb"
                wait = ([free_ws[mb % 2]] if free_ws[mb % 2]
                        else [b.tok("T")])
                wtok[mb] = wslice_dma(ws[:, 0:1024], w_dram, mb * 128, wait,
                                      dsem)

            plan_w(0)
            plan_w(1)
            res_toks = []
            prev_ev = [None, None]
            for mb in range(NB):
                ws = WSA if mb % 2 == 0 else WSB
                for s4 in range(4):
                    def mm(t, mb=mb, s4=s4, ws=ws):
                        for i in range(NB):
                            ins = t.matmul(
                                PC[:, (s4 % 2) * 512:(s4 % 2) * 512 + 512],
                                ws[:, i * 128:i * 128 + 128],
                                BIG[:, i * T + s4 * 512:i * T + s4 * 512 + 512],
                                start=(i == 0), stop=(i == NB - 1))
                        return ins
                    w = [wtok[mb]] + src_toks + \
                        ([prev_ev[s4 % 2]] if prev_ev[s4 % 2] else [])
                    tmm = b.op("tensor", mm, wait=w, sem="T", inc=1)
                    if bias_col is not None:
                        def ev(v, mb=mb, s4=s4):
                            return v.scalar_tensor_tensor(
                                XT[:, mb * T + s4 * 512:
                                   mb * T + s4 * 512 + 512],
                                PC[:, (s4 % 2) * 512:(s4 % 2) * 512 + 512],
                                bias_col[:, mb:mb + 1],
                                XT[:, mb * T + s4 * 512:
                                   mb * T + s4 * 512 + 512],
                                ALU.add, ALU.add)
                    else:
                        def ev(v, mb=mb, s4=s4):
                            return v.tensor_tensor(
                                XT[:, mb * T + s4 * 512:
                                   mb * T + s4 * 512 + 512],
                                XT[:, mb * T + s4 * 512:
                                   mb * T + s4 * 512 + 512],
                                PC[:, (s4 % 2) * 512:(s4 % 2) * 512 + 512],
                                ALU.add)
                    tev = b.op("vector", ev, wait=[tmm], sem=None, inc=0)
                    prev_ev[s4 % 2] = tev
                    if s4 == 3:
                        res_toks.append(tev)
                        if out_dma:
                            b.op("sync", lambda s, mb=mb: s.dma_start(
                                out=yT_d.rearrange("(g p) t -> p g t",
                                                   p=128)[:, mb, :],
                                in_=XT[:, mb * T:mb * T + T]),
                                wait=[tev], sem="dy", inc=16)
                free_ws[mb % 2] = b.tok("T")
                plan_w(mb + 2)
            ln_state["lnt_free"] = b.tok("T")
            return res_toks

        def emit_ffn(ln_toks, out_dma=False):
            free_ws = [None, None]
            wtok = [None] * (NB + 2)

            def plan_w(mb):
                if mb >= NB:
                    return
                ws = WSA if mb % 2 == 0 else WSB
                dsem = "dfa" if mb % 2 == 0 else "dfb"
                wait = ([free_ws[mb % 2]] if free_ws[mb % 2]
                        else [b.tok("T")])
                wtok[mb] = b.op("sync", lambda s, ws=ws, mb=mb: s.dma_start(
                    out=ws[:, 0:1024],
                    in_=wsgn_d[:, mb * 1024:mb * 1024 + 1024]),
                    wait=wait + [sgn_tok[mb]], sem=dsem, inc=16,
                    is_dma=True)

            plan_w(0)
            plan_w(1)
            res_toks = []
            prev_ev = [None, None]
            for mb in range(NB):
                ws = WSA if mb % 2 == 0 else WSB
                for s4 in range(4):
                    def mm(t, mb=mb, s4=s4, ws=ws):
                        for i in range(NB):
                            ins = t.matmul(
                                PC[:, (s4 % 2) * 512:(s4 % 2) * 512 + 512],
                                ws[:, i * 128:i * 128 + 128],
                                LNT[:, i * T + s4 * 512:i * T + s4 * 512 + 512],
                                start=(i == 0), stop=(i == NB - 1))
                        return ins
                    w = [wtok[mb], ln_toks[s4]] + \
                        ([prev_ev[s4 % 2]] if prev_ev[s4 % 2] else [])
                    tmm = b.op("tensor", mm, wait=w, sem="T", inc=1)
                    tev = b.op("vector",
                               lambda v, mb=mb, s4=s4: v.tensor_tensor(
                                   XT[:, mb * T + s4 * 512:
                                      mb * T + s4 * 512 + 512],
                                   XT[:, mb * T + s4 * 512:
                                      mb * T + s4 * 512 + 512],
                                   PC[:, (s4 % 2) * 512:(s4 % 2) * 512 + 512],
                                   ALU.add), wait=[tmm], sem=None, inc=0)
                    prev_ev[s4 % 2] = tev
                    if s4 == 3:
                        res_toks.append(tev)
                        if out_dma:
                            b.op("sync", lambda s, mb=mb: s.dma_start(
                                out=yT_d.rearrange("(g p) t -> p g t",
                                                   p=128)[:, mb, :],
                                in_=XT[:, mb * T:mb * T + T]),
                                wait=[tev], sem="dy", inc=16)
                free_ws[mb % 2] = b.tok("T")
                plan_w(mb + 2)
            ln_state["lnt_free"] = b.tok("T")
            return res_toks

        def emit_attnb(ln_toks):
            free_ws = [None, None]
            wtok = [None] * (NB + 2)

            def plan_w(p):
                if p >= NB:
                    return
                ws = WSA if p % 2 == 0 else WSB
                dsem = "dwa" if p % 2 == 0 else "dwb"
                wait = ([free_ws[p % 2]] if free_ws[p % 2]
                        else [b.tok("T")])
                wslice_dma(ws[:, 0:1024], wqT_d, p * 128, wait, dsem)
                wslice_dma(ws[:, 1024:2048], wkbT_d, p * 128, [], dsem)
                wtok[p] = wslice_dma(ws[:, 2048:3072], wvbT_d, p * 128, [],
                                     dsem)

            plan_w(0)
            plan_w(1)
            big_toks = []
            pair_end_T = [None, None]
            qkv_toks = [[], []]
            pr_free = list(ln_state["pr_last"])

            def make_qkv_units(p):
                if p >= NB:
                    return []
                pi = p % 2
                ws = WSA if pi == 0 else WSB
                qt, kt, vsb = QT2[pi], KT2[pi], VSB2[pi]
                wprev = pair_end_T[pi]
                units = []
                toks = qkv_toks[pi]
                toks.clear()

                def u_memset():
                    tk = b.op("vector", lambda v, vsb=vsb: v.memset(
                        vsb[:], 1.0), wait=[wprev], sem=None, inc=0)
                    toks.append(tk)
                units.append(u_memset)
                for s4 in range(4):
                    def u_q(s4=s4):
                        prr = PR if s4 % 2 == 0 else PR2
                        def mm(t, s4=s4, prr=prr):
                            for i in range(NB):
                                ins = t.matmul(
                                    prr[:], ws[:, i * 128:i * 128 + 128],
                                    LNT[:, i * T + s4 * 512:
                                        i * T + s4 * 512 + 512],
                                    start=(i == 0), stop=(i == NB - 1))
                            return ins
                        tq = b.op("tensor", mm,
                                  wait=[wtok[p], pr_free[s4 % 2], wprev,
                                        ln_toks[s4]], sem="T", inc=1)
                        ev = b.op("vector", lambda v, s4=s4, prr=prr:
                                  v.tensor_scalar(
                                      qt[:, s4 * 512:s4 * 512 + 512], prr[:],
                                      0.125, DER[:, p:p + 1],
                                      ALU.mult, ALU.add),
                                  wait=[tq, wprev], sem=None, inc=0)
                        pr_free[s4 % 2] = ev
                        toks.append(ev)
                    units.append(u_q)
                for s4 in range(4):
                    def u_k(s4=s4):
                        prr = PR if s4 % 2 == 0 else PR2
                        def mm(t, s4=s4, prr=prr):
                            for i in range(NB):
                                ins = t.matmul(
                                    prr[:],
                                    ws[:, 1024 + i * 128:1024 + i * 128 + 128],
                                    LNT[:, i * T + s4 * 512:
                                        i * T + s4 * 512 + 512],
                                    start=(i == 0), stop=(i == NB - 1))
                            return ins
                        tk_ = b.op("tensor", mm,
                                   wait=[wtok[p], pr_free[s4 % 2], wprev,
                                         ln_toks[s4]], sem="T", inc=1)
                        ev = b.op("vector", lambda v, s4=s4, prr=prr:
                                  v.tensor_single_scalar(
                                      kt[:, s4 * 512:s4 * 512 + 512], prr[:],
                                      VECS[:, 8 + p:8 + p + 1], ALU.add),
                                  wait=[tk_, wprev], sem=None, inc=0)
                        pr_free[s4 % 2] = ev
                        toks.append(ev)
                    units.append(u_k)
                for tt in range(16):
                    def u_v(tt=tt):
                        prr = PR if tt % 2 == 0 else PR2
                        def mm(t, tt=tt, prr=prr):
                            for i in range(NB):
                                t.matmul(
                                    prr[:, 0:128],
                                    LNT[:, i * T + tt * 128:
                                        i * T + tt * 128 + 128],
                                    ws[:, 2048 + i * 128:2048 + i * 128 + 128],
                                    start=(i == 0), stop=False)
                            return t.matmul(
                                prr[:, 0:128], ONESRB[0:1, :],
                                BVR[0:1, p * 128:p * 128 + 128],
                                start=False, stop=True)
                        tvm = b.op("tensor", mm,
                                   wait=[wtok[p], pr_free[tt % 2], tok_bvr,
                                         toks[0], ln_toks[tt // 4]],
                                   sem="T", inc=1)
                        ev = b.op("vector", lambda v, tt=tt, prr=prr:
                                  v.tensor_copy(
                                      vsb.rearrange(
                                          "p (t h c) -> p t h c", h=2, c=65)
                                      [:, tt, :, 0:64],
                                      prr[:, 0:128].rearrange(
                                          "p (h c) -> p h c", c=64)),
                                  wait=[tvm], sem=None, inc=0)
                        pr_free[tt % 2] = ev
                        toks.append(ev)
                    units.append(u_v)
                return units

            for u in make_qkv_units(0):
                u()
            pending = list(make_qkv_units(1))

            norm_done = None
            pending_norm = [None]
            for p in range(NB):
                pi = p % 2
                QTp, KTp, VSB = QT2[pi], KT2[pi], VSB2[pi]
                qt_all = list(qkv_toks[pi]) + [tok_consts]
                prev_qs_exp = pair_end_T[1 - pi]
                prev_qs_av = pair_end_T[1 - pi]
                for qs in range(4):
                    av_toks = {}
                    exp_toks = []
                    nkt = 4 * qs + 4
                    avq = []

                    def plan_av(kt, qs=qs, nkt=nkt, VSB=VSB,
                                av_toks=av_toks):
                        diag = kt >= 4 * qs
                        qoff = 128 * kt if diag else 512 * qs
                        N = (512 * (qs + 1) - 128 * kt) if diag else 512
                        ep = (EPA, EPB, EPC, EPD)[kt % 4]

                        def mm_av(t, kt=kt, qs=qs, ep=ep, qoff=qoff, N=N,
                                  nkt=nkt, VSB=VSB):
                            o = qoff - 512 * qs
                            t.matmul(PO0[0:65, o:o + N],
                                     VSB[:, kt * 130:kt * 130 + 65],
                                     ep[:, 0:N],
                                     start=(kt == 0), stop=(kt == nkt - 1))
                            return t.matmul(
                                PO1[0:65, o:o + N],
                                VSB[:, kt * 130 + 65:kt * 130 + 130],
                                ep[:, 512:512 + N],
                                start=(kt == 0), stop=(kt == nkt - 1))
                        wav = [exp_toks[kt]]
                        if kt == 0 and norm_done is not None:
                            wav.append(norm_done)
                        av_toks[kt] = b.op("tensor", mm_av, wait=wav,
                                           sem="T", inc=1)

                    for kc in range(0, nkt, 2):
                      for kt in (kc, kc + 1):
                        diag = kt >= 4 * qs
                        qoff = 128 * kt if diag else 512 * qs
                        N = (512 * (qs + 1) - 128 * kt) if diag else 512
                        ps = PC if kt % 2 == 0 else PD
                        def mm_s(t, ps=ps, kt=kt, qoff=qoff, N=N, diag=diag,
                                 QTp=QTp, KTp=KTp):
                            t.matmul(ps[:, 0:N],
                                     KTp[0:64, kt * 128:kt * 128 + 128],
                                     QTp[0:64, qoff:qoff + N],
                                     start=True, stop=not diag,
                                     tile_position=(0, 0))
                            ins = t.matmul(
                                ps[:, 512:512 + N],
                                KTp[64:128, kt * 128:kt * 128 + 128],
                                QTp[64:128, qoff:qoff + N],
                                start=True, stop=not diag,
                                tile_position=(64, 0))
                            if diag:
                                t.matmul(ps[:, 0:128], IDENTB[:], MASKB[:],
                                         start=False, stop=True)
                                ins = t.matmul(ps[:, 512:640], IDENTB[:],
                                               MASKB[:], start=False,
                                               stop=True)
                            return ins
                        w = list(qt_all)
                        if kt >= 2:
                            w.append(exp_toks[kt - 2])
                        elif prev_qs_exp is not None:
                            w.append(prev_qs_exp)
                        ts_ = b.op("tensor", mm_s, wait=w, sem="T", inc=1)
                        exp_toks.append([ts_, None])
                      if pending:
                          pending.pop(0)()
                      while avq and avq[0] <= kc - 1:
                          plan_av(avq.pop(0))
                      for kt in (kc, kc + 1):
                        diag = kt >= 4 * qs
                        qoff = 128 * kt if diag else 512 * qs
                        N = (512 * (qs + 1) - 128 * kt) if diag else 512
                        ps = PC if kt % 2 == 0 else PD
                        ep = (EPA, EPB, EPC, EPD)[kt % 4]
                        wexp = [exp_toks[kt][0]]
                        if kt >= 4:
                            wexp.append(av_toks[kt - 4])
                        elif prev_qs_av is not None:
                            wexp.append(prev_qs_av)
                        if N == 512:
                            def do_exp(s, ps=ps, ep=ep):
                                return s.activation(
                                    ep[:, 0:1024], ps[:, 0:1024], AF.Exp)
                        else:
                            def do_exp(s, ps=ps, ep=ep, N=N):
                                return s.activation(
                                    ep.rearrange("p (h n) -> p h n", h=2)
                                    [:, :, 0:N],
                                    ps[:].rearrange("p (h n) -> p h n", h=2)
                                    [:, :, 0:N], AF.Exp)
                        te = b.op("scalar", do_exp, wait=wexp, sem=None,
                                  inc=0)
                        exp_toks[kt] = te
                        avq.append(kt)
                      if pending_norm[0] is not None:
                          norm_done = pending_norm[0]()
                          pending_norm[0] = None
                    while avq:
                        plan_av(avq.pop(0))
                    prev_qs_exp = exp_toks[-1]
                    prev_qs_av = av_toks[nkt - 1]
                    def make_norm(p=p, qs=qs, av_last=av_toks[nkt - 1]):
                        def run():
                            nts = []
                            for hh, po in ((0, PO0), (1, PO1)):
                                REC = REC2[hh]
                                RECB = RECB2[hh]
                                OSCR = OSCR2[hh]
                                r0 = b.op("scalar", lambda s, po=po, REC=REC:
                                          s.activation(
                                              REC, po[64:65, 0:512],
                                              AF.Ln), wait=[av_last],
                                          sem=None, inc=0)
                                r1 = b.op("scalar", lambda s, REC=REC,
                                          RECB=RECB: s.activation(
                                              RECB, REC, AF.Exp,
                                              scale=-1.0),
                                          wait=[r0], sem=None, inc=0)
                                c1 = b.op("vector", lambda v, po=po,
                                          OSCR=OSCR: v.tensor_copy(
                                              OSCR, po[0:64, 0:512]),
                                          wait=[av_last], sem=None, inc=0)
                                tr1 = b.op("tensor", lambda t, po=po,
                                           RECB=RECB: t.matmul(
                                               po[0:64, 0:512],
                                               ONESRB[0:1, 0:64], RECB,
                                               start=True, stop=True),
                                           wait=[r1, c1], sem="T", inc=1)
                                nt = b.op("vector", lambda v, p=p, qs=qs,
                                          hh=hh, po=po, OSCR=OSCR:
                                          v.tensor_tensor(
                                              BIG[hh * 64:hh * 64 + 64,
                                                  p * T + qs * 512:
                                                  p * T + qs * 512 + 512],
                                              OSCR, po[0:64, 0:512],
                                              ALU.mult),
                                          wait=[tr1], sem=None, inc=0)
                                nts.append(nt)
                                big_toks.append(nt)
                            fin = b.op("vector", lambda v, nts=nts:
                                       v.tensor_copy(REC2[0], REC2[0]),
                                       wait=nts, sem=None, inc=0)
                            return fin
                        return run
                    pending_norm[0] = make_norm()
                while pending:
                    pending.pop(0)()
                if p == NB - 1 and pending_norm[0] is not None:
                    norm_done = pending_norm[0]()
                    pending_norm[0] = None
                pair_end_T[pi] = b.tok("T")
                free_ws[pi] = b.tok("T")
                plan_w(p + 2)
                pending = list(make_qkv_units(p + 2))
            ln_state["lnt_free"] = b.tok("T")
            res = emit_proj(big_toks, wobT_d, bias_col=VECS[:, 16:24])
            return res

        streamed_out = [False]
        done = tok_x_list
        ln_out = None
        for si in range(stop_idx):
            st = STAGES[si]
            if st.startswith("ln"):
                ln_out = emit_ln(done, four_way=(si > 0))
                done = ln_out
            elif st == "attna":
                done = emit_attna(ln_out)
            elif st == "ffn1" or st == "ffn2":
                done = emit_ffn(ln_out, out_dma=(st == "ffn2"))
                if st == "ffn2":
                    streamed_out[0] = True
            elif st == "attnb":
                done = emit_attnb(ln_out)

        final_waits = [b.tok("T"), b.tok("VC"), b.tok("SC"), b.tok("GC")]
        if not streamed_out[0]:
            b.op("sync", lambda s: s.dma_start(
                out=yT_d.rearrange("(g p) t -> p g t", p=128),
                in_=XT[:].rearrange("p (g t) -> p g t", t=T)),
                wait=final_waits, sem="dy", inc=16)
        if stop_after is not None:
            b.op("sync", lambda s: s.dma_start(out=dXT, in_=XT[:]),
                 wait=final_waits, sem="dy", inc=16)
            b.op("gpsimd", lambda g: g.dma_start(out=dLNT, in_=LNT[:]),
                 wait=final_waits, sem="dz", inc=16, is_dma=True)
            b.op("gpsimd", lambda g: g.dma_start(out=dBIG, in_=BIG[:]),
                 wait=final_waits, sem="dz", inc=16, is_dma=True)
            b.op("gpsimd", lambda g: g.dma_start(out=dPOOL, in_=POOL[:]),
                 wait=final_waits, sem="dz", inc=16, is_dma=True)
        b.op("sync", lambda s: s.nop(), wait=[b.tok("dy"), b.tok("dz")])

        block = e(nc.Block())
        b.emit(block, sems)

    return nc


def _prep_inputs(inputs):
    f32 = np.float32
    x = np.asarray(inputs["L0_attn_a_x"])
    wk = np.ascontiguousarray(np.asarray(inputs["L0_attn_a_Wk"], dtype=f32))
    wv = np.ascontiguousarray(np.asarray(inputs["L0_attn_a_Wv"], dtype=f32))
    woa = np.ascontiguousarray(np.asarray(inputs["L0_attn_a_Wo"], dtype=f32))
    gate = np.asarray(inputs["L0_attn_a_gate"], dtype=f32)
    freqs = np.asarray(inputs["L0_attn_a_freqs"], dtype=f32)
    wq = np.asarray(inputs["L0_attn_b_Wq"], dtype=f32)
    bq = np.asarray(inputs["L0_attn_b_bq"], dtype=f32)
    wkb = np.asarray(inputs["L0_attn_b_Wk"], dtype=f32)
    bk = np.asarray(inputs["L0_attn_b_bk"], dtype=f32)
    wvb = np.asarray(inputs["L0_attn_b_Wv"], dtype=f32)
    bv = np.asarray(inputs["L0_attn_b_bv"], dtype=f32)
    wob = np.asarray(inputs["L0_attn_b_Wo"], dtype=f32)
    bo = np.asarray(inputs["L0_attn_b_bo"], dtype=f32)
    wffn = np.asarray(inputs["L0_ffn_weight"], dtype=f32)

    shared = {
        "wk": wk, "wv": wv, "woa": woa,
        "wts": np.ascontiguousarray(wffn.T),
        "wqT": np.ascontiguousarray(wq.T),
        "wkbT": np.ascontiguousarray(wkb.T),
        "wvbT": np.ascontiguousarray(wvb.T),
        "wobT": np.ascontiguousarray(wob.T),
    }
    col = lambda a: np.ascontiguousarray(np.asarray(a).reshape(8, 128).T)
    shared["vecs"] = np.concatenate([
        col(bq), col(bk), col(bo),
        col(np.repeat(gate[:, 0], 64)), col(np.repeat(gate[:, 1], 64)),
        col(freqs.reshape(-1)),
    ], axis=1).astype(f32)
    shared["bvrow"] = np.ascontiguousarray(bv.reshape(1, D))
    shared["trow"] = np.ascontiguousarray(np.broadcast_to(
        np.arange(1024, dtype=f32)[None, :], (128, 1024)))
    isq = 1.0 / np.sqrt(np.arange(1, T + 1, dtype=np.float64))
    shared["isq"] = np.ascontiguousarray(np.broadcast_to(
        isq[None, :], (128, T))).astype(ml_dtypes.bfloat16)
    ki, qi = np.meshgrid(np.arange(128), np.arange(128), indexing="ij")
    shared["identb"] = np.ascontiguousarray(
        np.eye(128, dtype=ml_dtypes.bfloat16))
    shared["maskb"] = np.ascontiguousarray(
        np.where(ki > qi, -1e30, 0.0).astype(ml_dtypes.bfloat16))

    in_maps = []
    for bi in range(8):
        m = dict(shared)
        m["xT"] = np.ascontiguousarray(np.asarray(x[bi], dtype=f32).T)
        in_maps.append(m)
    return in_maps


def kernel(**inputs):
    in_maps = _prep_inputs(inputs)
    nc = build_program()
    res = bass_utils.run_bass_kernel_spmd(nc, in_maps, core_ids=list(range(8)))
    out = np.empty((8, T, D), dtype=np.float32)
    for bi in range(8):
        out[bi] = np.asarray(res.results[bi]["yT"]).T
    return out


# version 70
# speedup vs baseline: 1.0227x; 1.0227x over previous
import contextlib
import numpy as np
import ml_dtypes
import concourse.bass as bass
import concourse.mybir as mybir
import concourse.bass_utils as bass_utils

F32 = mybir.dt.float32
BF16 = mybir.dt.bfloat16
I32 = mybir.dt.int32
AF = mybir.ActivationFunctionType
ALU = mybir.AluOpType
AX = mybir.AxisListType

T = 2048
D = 1024
NB = 8
PH = 10.0
TPI = float(2 * np.pi)
ENGINES = ("sync", "tensor", "vector", "scalar", "gpsimd")
STAGES = ["ln1", "attna", "ln2", "ffn1", "ln3", "attnb", "ln4", "ffn2"]


class Builder:
    CHAIN = {"vector": "VC", "scalar": "SC", "gpsimd": "GC"}

    def __init__(self):
        self.events = {e: [] for e in ENGINES}
        self.cnt = {}

    def tok(self, sem):
        return (sem, self.cnt.get(sem, 0))

    def op(self, eng, fn, wait=(), sem=None, inc=0, is_dma=False):
        waits = [w for w in wait if w is not None]
        chain = self.CHAIN.get(eng) if not is_dma else None
        self.events[eng].append((fn, waits, sem, inc, chain))
        tok = None
        if chain:
            self.cnt[chain] = self.cnt.get(chain, 0) + 1
            tok = (chain, self.cnt[chain])
        if inc:
            self.cnt[sem] = self.cnt.get(sem, 0) + inc
            tok = (sem, self.cnt[sem])
        return tok

    def emit(self, block, sems):
        def runner(eng_name):
            def run(eng):
                hi = {}
                chain_pos = {}
                for fn, waits, sem, inc, chain in self.events[eng_name]:
                    if chain:
                        k = chain_pos.get(chain, 0)
                        if k > 0:
                            eng.wait_ge(sems[chain], k)
                        chain_pos[chain] = k + 1
                    for ws, wv in waits:
                        if wv > 0 and hi.get(ws, 0) < wv:
                            eng.wait_ge(sems[ws], wv)
                            hi[ws] = wv
                    ins = fn(eng)
                    if inc:
                        ins.then_inc(sems[sem], inc)
                    elif chain:
                        ins.then_inc(sems[chain], 1)
                return None
            return run
        block.sync(runner("sync"))
        block.tensor(runner("tensor"))
        block.vector(runner("vector"))
        block.scalar(runner("scalar"))
        block.gpsimd(runner("gpsimd"))


def build_program(stop_after=None):
    stop_idx = STAGES.index(stop_after) + 1 if stop_after else len(STAGES)

    nc = bass.Bass("TRN2", target_bir_lowering=False)

    xT_d = nc.dram_tensor("xT", [D, T], F32, kind="ExternalInput").ap()
    wk_d = nc.dram_tensor("wk", [D, D], F32, kind="ExternalInput").ap()
    wv_d = nc.dram_tensor("wv", [D, D], F32, kind="ExternalInput").ap()
    woa_d = nc.dram_tensor("woa", [D, D], F32, kind="ExternalInput").ap()
    wts_d = nc.dram_tensor("wts", [D, D], F32, kind="ExternalInput").ap()
    wsgn_d = nc.dram_tensor("wsgn", [128, 8192], BF16,
                            kind="Internal").ap()
    wqT_d = nc.dram_tensor("wqT", [D, D], F32, kind="ExternalInput").ap()
    wkbT_d = nc.dram_tensor("wkbT", [D, D], F32, kind="ExternalInput").ap()
    wvbT_d = nc.dram_tensor("wvbT", [D, D], F32, kind="ExternalInput").ap()
    wobT_d = nc.dram_tensor("wobT", [D, D], F32, kind="ExternalInput").ap()
    vecs_d = nc.dram_tensor("vecs", [128, 48], F32, kind="ExternalInput").ap()
    bvrow_d = nc.dram_tensor("bvrow", [1, D], F32, kind="ExternalInput").ap()
    trow_d = nc.dram_tensor("trow", [128, 1024], F32, kind="ExternalInput").ap()
    isq_d = nc.dram_tensor("isq", [128, T], BF16, kind="ExternalInput").ap()
    identb_d = nc.dram_tensor("identb", [128, 128], BF16,
                              kind="ExternalInput").ap()
    maskb_d = nc.dram_tensor("maskb", [128, 128], BF16,
                             kind="ExternalInput").ap()
    yT_d = nc.dram_tensor("yT", [D, T], F32, kind="ExternalOutput").ap()
    if stop_after is not None:
        dXT = nc.dram_tensor("dbg_XT", [128, NB * T], F32,
                             kind="ExternalOutput").ap()
        dLNT = nc.dram_tensor("dbg_LNT", [128, NB * T], F32,
                              kind="ExternalOutput").ap()
        dBIG = nc.dram_tensor("dbg_BIG", [128, NB * T], F32,
                              kind="ExternalOutput").ap()
        dPOOL = nc.dram_tensor("dbg_POOL", [128, 20480], F32,
                               kind="ExternalOutput").ap()

    with contextlib.ExitStack() as stack:
        e = stack.enter_context

        XT = e(nc.sbuf_tensor("XT", [128, NB * T], F32))
        LNT = e(nc.sbuf_tensor("LNT", [128, NB * T], BF16))
        BIG = e(nc.sbuf_tensor("BIG", [128, NB * T], BF16))
        WSA = e(nc.sbuf_tensor("WSA", [128, 3 * 1024], BF16))
        WSB = e(nc.sbuf_tensor("WSB", [128, 3 * 1024], BF16))
        POOL = e(nc.sbuf_tensor("POOLBF", [128, 20480], BF16))
        PF = e(nc.sbuf_tensor("PF32", [128, 3072], F32))
        TROW = e(nc.sbuf_tensor("TROW", [128, 1024], F32))
        ISQ = e(nc.sbuf_tensor("ISQ", [128, T], BF16))
        IDENTB = e(nc.sbuf_tensor("IDENTB", [128, 128], BF16))
        MASKB = e(nc.sbuf_tensor("MASKB", [128, 128], BF16))
        VECS = e(nc.sbuf_tensor("VECS", [128, 48], F32))
        DER = e(nc.sbuf_tensor("DER", [128, 24], F32))
        BVR = e(nc.sbuf_tensor("BVR", [1, D], BF16))
        ONESB = e(nc.sbuf_tensor("ONESB", [128, 1], BF16))
        B512 = e(nc.sbuf_tensor("B512", [128, 1], F32))
        HPI = e(nc.sbuf_tensor("HPI", [128, 1], F32))
        EPSC = e(nc.sbuf_tensor("EPSC", [128, 1], F32))
        ONESRB = e(nc.sbuf_tensor("ONESRB", [33, 128], BF16))
        INT = e(nc.sbuf_tensor("INTP", [128, 1024], I32))

        def pv(c0, n):
            return POOL[:, c0:c0 + n]
        CPH = pv(0, T); SPH = pv(T, T)
        CTH2_ = (pv(2 * T, T), pv(6 * T, T))
        STH2_ = (pv(3 * T, T), pv(9 * T, T))
        VBF = pv(4 * T, T)
        PRD = pv(5 * T, T)
        SC1 = pv(7 * T, T); SC2 = pv(8 * T, T)
        SQF = POOL[:, 0:NB * T]
        RB = pv(NB * T, T)
        MRB = pv(NB * T + T, T)
        QT2 = (pv(0, T), pv(12288, T))
        KT2 = (pv(T, T), pv(14336, T))
        VSB2 = (pv(2 * T, 2080), pv(16384, 2080))
        EPA = pv(2 * T + 2080, 1024)
        EPB = pv(2 * T + 2080 + 1024, 1024)
        EPC = pv(10240, 1024)
        EPD = pv(11264, 1024)

        U = PF[:, 0:1024]
        UF = PF[:, 1024:2048]
        FS = PF[:, 2048:3072]
        FSB = (FS, INT[:].bitcast(F32))
        REC2 = (PF[0:1, 2048:2560], PF[0:1, 0:512])
        RECB2 = (PF[0:32, 2048:2560].bitcast(BF16)[0:1, 0:512],
                 PF[0:32, 0:512].bitcast(BF16)[0:1, 0:512])
        OSCR2 = (PF[0:64, 2560:3072], PF[0:64, 1024:1536])

        PC = e(nc.psum_tensor("PC", [128, 1024], F32))
        PD = e(nc.psum_tensor("PD", [128, 1024], F32))
        POB = e(nc.psum_tensor("POB", [128, 1024], F32))
        PO0 = POB[0:65, 0:512]
        PO1 = POB[0:65, 512:1024]
        PRB = e(nc.psum_tensor("PRB", [128, 1024], F32))
        PR = PRB[:, 0:512]
        PR2 = PRB[:, 512:1024]
        UB2 = (U, POB[:], PRB[:])

        sem_names = ["T", "VC", "SC", "GC", "dx", "dc", "dw", "dwa", "dwb", "dy", "dz", "dsi", "dso", "dfa", "dfb", "dx2"]
        sems = {n: e(nc.semaphore("sem_" + n)) for n in sem_names}

        b = Builder()

        for _cb in range(4):
            b.op("sync", lambda s, _cb=_cb: s.dma_start(
                out=XT[:, _cb * T:_cb * T + T],
                in_=xT_d.rearrange("(g p) t -> p g t", p=128)[:, _cb, :]),
                sem="dx", inc=16)
        tok_xa = b.tok("dx")
        for _cb in range(4, NB):
            b.op("sync", lambda s, _cb=_cb: s.dma_start(
                out=XT[:, _cb * T:_cb * T + T],
                in_=xT_d.rearrange("(g p) t -> p g t", p=128)[:, _cb, :]),
                sem="dx2", inc=16)
        tok_x = b.tok("dx2")
        tok_x_list = [tok_xa] * 4 + [tok_x] * 4
        for dst, src in ((VECS, vecs_d), (TROW, trow_d), (ISQ, isq_d),
                         (IDENTB, identb_d), (MASKB, maskb_d)):
            b.op("sync", lambda s, dst=dst, src=src: s.dma_start(
                out=dst[:], in_=src), sem="dc", inc=16)
        b.op("gpsimd", lambda g: g.dma_start(out=BVR[:], in_=bvrow_d),
             sem="dw", inc=16, is_dma=True)
        tok_consts = b.tok("dc")
        tok_bvr = b.tok("dw")
        b.op("vector", lambda v: v.memset(ONESB[:], 1.0))
        b.op("vector", lambda v: v.memset(B512[:], 512.0))
        b.op("vector", lambda v: v.memset(HPI[:], 1.5707963267948966))
        b.op("vector", lambda v: v.memset(EPSC[:], 1e-5))
        b.op("vector", lambda v: v.memset(ONESRB[0:33, :], 1.0))
        b.op("vector", lambda v: v.tensor_single_scalar(
            DER[:, 0:8], VECS[:, 0:8], 0.125, ALU.mult), wait=[tok_consts])
        b.op("vector", lambda v: v.tensor_single_scalar(
            DER[:, 8:16], VECS[:, 40:48], 1.0 / TPI, ALU.mult))
        tok_der = b.op("vector", lambda v: v.tensor_single_scalar(
            DER[:, 16:24], DER[:, 8:16], 1024.0, ALU.mult),
            sem=None, inc=0)

        def wslice_dma(dst, src_d, col0, tokens, dsem):
            def fn(g):
                return g.dma_start(
                    out=dst.rearrange("p (g c) -> p g c", c=128),
                    in_=src_d[:, col0:col0 + 128].rearrange(
                        "(g p) c -> p g c", p=128))
            return b.op("gpsimd", fn, wait=tokens, sem=dsem, inc=16,
                        is_dma=True)

        SGA = BIG[:, 7 * T:7 * T + 1024]
        SGB = BIG[:, 7 * T + 1024:8 * T]
        sgn_tok = [None] * NB
        sgn_all_l = [None]

        _sg_prev_sign = [None]
        _sg_prev_out = [None]

        def plan_sign_chunk(_mb):
            w_in = ([_sg_prev_sign[0]] if _sg_prev_sign[0] else [])
            t_in = b.op("gpsimd", lambda g, _mb=_mb: g.dma_start(
                out=SGA.rearrange("p (g c) -> p g c", c=128),
                in_=wts_d[:, _mb * 128:_mb * 128 + 128].rearrange(
                    "(g p) c -> p g c", p=128)),
                wait=w_in, sem="dsi", inc=16, is_dma=True)
            w_s = [t_in] + ([_sg_prev_out[0]] if _sg_prev_out[0] else [])
            t_s = b.op("scalar", lambda s: s.activation(
                SGB[:], SGA[:], AF.Sign), wait=w_s, sem=None, inc=0)
            _sg_prev_sign[0] = t_s
            t_o = b.op("sync", lambda s, _mb=_mb: s.dma_start(
                out=wsgn_d[:, _mb * 1024:_mb * 1024 + 1024],
                in_=SGB[:]), wait=[t_s], sem="dso", inc=16)
            _sg_prev_out[0] = t_o
            sgn_tok[_mb] = t_o
            if _mb == NB - 1:
                sgn_all_l[0] = b.tok("dso")

        ln_state = {"pr_last": [None, None], "po_reads": [],
                    "lnt_free": None}

        def emit_ln(prev_toks, four_way=False):
            copy_toks = []
            tok_tprev = (ln_state["lnt_free"]
                         if ln_state["lnt_free"] is not None
                         else b.tok("T"))
            for cb in range(NB):
                w = ([prev_toks[cb]] if len(prev_toks) == NB
                     else list(prev_toks))
                if cb % 2 == 0:
                    tk = b.op("vector", lambda v, cb=cb: v.tensor_copy(
                        LNT[:, cb * T:(cb + 1) * T],
                        XT[:, cb * T:(cb + 1) * T]),
                        wait=w + [tok_tprev], sem=None, inc=0)
                else:
                    tk = b.op("scalar", lambda s, cb=cb: s.activation(
                        LNT[:, cb * T:(cb + 1) * T],
                        XT[:, cb * T:(cb + 1) * T], AF.Copy),
                        wait=w + [tok_tprev], sem=None, inc=0)
                copy_toks.append(tk)
            sq_toks = []
            for cb in range(NB):
                if cb % 2 == 0:
                    tk = b.op("scalar", lambda s, cb=cb: s.activation(
                        SQF[:, cb * T:(cb + 1) * T],
                        LNT[:, cb * T:(cb + 1) * T],
                        AF.Square), wait=[copy_toks[cb]], sem=None, inc=0)
                else:
                    tk = b.op("vector", lambda v, cb=cb: v.tensor_tensor(
                        SQF[:, cb * T:(cb + 1) * T],
                        LNT[:, cb * T:(cb + 1) * T],
                        LNT[:, cb * T:(cb + 1) * T], ALU.mult),
                        wait=[copy_toks[cb]], sem=None, inc=0)
                sq_toks.append(tk)
            def mk_slots(p0):
                return (PF[p0:p0 + 1, 0:512], PF[p0:p0 + 1, 512:1024],
                        PF[p0:p0 + 1, 1024:1536],
                        PF[p0:p0 + 32, 1024:1536].bitcast(BF16)[0:1, 0:512],
                        PF[p0:p0 + 32, 1536:2048].bitcast(BF16)[0:1, 0:512],
                        ONESRB[p0:p0 + 1, 0:128], EPSC[p0:p0 + 1, 0:1])

            def mk_slots_i(p0):
                return (PF[p0:p0 + 1, 2048:2560],
                        PF[p0:p0 + 1, 2560:3072],
                        INT[p0:p0 + 1, 0:512].bitcast(F32),
                        INT[p0:p0 + 32, 0:512].bitcast(BF16)[0:1, 0:512],
                        INT[p0:p0 + 32, 512:1024].bitcast(BF16)
                        [0:1, 0:512],
                        ONESRB[p0:p0 + 1, 0:128], EPSC[p0:p0 + 1, 0:1])
            if four_way:
                SLOTS = (mk_slots(0), mk_slots(32),
                         mk_slots_i(0), mk_slots_i(32))
            else:
                SLOTS = (mk_slots(0), mk_slots(32))
            ev_toks = []
            out_toks = []
            apply_pending = [None]
            PBS = (0, 32, 64, 0)
            t12 = []

            def plan_stats(s4, extra):
                pb = PBS[s4]
                def mm_x(t, s4=s4, pb=pb):
                    for cb in range(NB):
                        ins = t.matmul(PO0[pb:pb + 1, :], ONESB[:],
                                       LNT[:, cb * T + s4 * 512:
                                           cb * T + s4 * 512 + 512],
                                       start=(cb == 0), stop=(cb == NB - 1))
                    return ins
                def mm_x2(t, s4=s4, pb=pb):
                    for cb in range(NB):
                        ins = t.matmul(PO1[pb:pb + 1, :], ONESB[:],
                                       SQF[:, cb * T + s4 * 512:
                                           cb * T + s4 * 512 + 512],
                                       start=(cb == 0), stop=(cb == NB - 1))
                    return ins
                t1 = b.op("tensor", mm_x, wait=copy_toks + extra,
                          sem="T", inc=1)
                t2 = b.op("tensor", mm_x2, wait=sq_toks + extra,
                          sem="T", inc=1)
                t12.append((t1, t2))

            for s4 in range(3):
                plan_stats(s4, [])
            pr_done = [None, None]
            prev_tpr1 = [None] * len(SLOTS)
            prev_tpr2 = [None] * len(SLOTS)
            for s4 in range(4):
                par = s4 % len(SLOTS)
                pb = PBS[s4]
                m_sl, s2_sl, r_sl, rb_sl, mrb_sl, ones, eps = SLOTS[par]
                t1, t2 = t12[s4]
                tvm = b.op("scalar", lambda s, pb=pb, m_sl=m_sl:
                           s.activation(
                               m_sl, PO0[pb:pb + 1, :], AF.Copy,
                               scale=1.0 / D),
                           wait=[t1], sem=None, inc=0)
                tvs = b.op("scalar", lambda s, pb=pb, s2_sl=s2_sl:
                           s.activation(
                               s2_sl, PO1[pb:pb + 1, :], AF.Copy,
                               scale=1.0 / D),
                           wait=[t2], sem=None, inc=0)
                ln_state["po_reads"] = [tvm, tvs]
                if s4 == 0:
                    plan_stats(3, [tvm, tvs])
                b.op("vector", lambda v, m_sl=m_sl, r_sl=r_sl:
                     v.tensor_tensor(r_sl, m_sl, m_sl, ALU.mult),
                     wait=[tvm, prev_tpr1[par]])
                tvar = b.op("vector", lambda v, s2_sl=s2_sl, r_sl=r_sl:
                            v.tensor_sub(s2_sl, s2_sl, r_sl),
                            wait=[tvs], sem=None, inc=0)
                if apply_pending[0] is not None:
                    apply_pending[0]()
                    apply_pending[0] = None
                tln = b.op("scalar", lambda s, r_sl=r_sl, s2_sl=s2_sl,
                           eps=eps: s.activation(
                               r_sl, s2_sl, AF.Ln, bias=eps),
                           wait=[tvar], sem=None, inc=0)
                trr = b.op("scalar", lambda s, rb_sl=rb_sl, r_sl=r_sl:
                           s.activation(rb_sl, r_sl, AF.Exp, scale=-0.5),
                           wait=[tln], sem=None, inc=0)

                def mk_tail(s4=s4, par=par, trr=trr, m_sl=m_sl,
                            rb_sl=rb_sl, mrb_sl=mrb_sl, ones=ones):
                    def run():
                        tmr = b.op("vector", lambda v: v.tensor_tensor(
                            mrb_sl, m_sl, rb_sl, ALU.mult),
                            wait=[trr, prev_tpr2[par]], sem=None, inc=0)
                        wpr = [trr] + ([pr_done[0]] if pr_done[0] else [])
                        tpr1 = b.op("tensor", lambda t: t.matmul(
                            PR[:], ones, rb_sl, start=True, stop=True),
                            wait=wpr, sem="T", inc=1)
                        e1 = b.op("scalar", lambda s: s.activation(
                            RB[:, s4 * 512:s4 * 512 + 512], PR[:],
                            AF.Copy), wait=[tpr1], sem=None, inc=0)
                        tpr2 = b.op("tensor", lambda t: t.matmul(
                            PR2[:], ones, mrb_sl, start=True, stop=True),
                            wait=[tmr] + ([pr_done[1]] if pr_done[1]
                                          else []), sem="T", inc=1)
                        e2 = b.op("scalar", lambda s: s.activation(
                            MRB[:, s4 * 512:s4 * 512 + 512], PR2[:],
                            AF.Copy), wait=[tpr2], sem=None, inc=0)
                        prev_tpr1[par] = tpr1
                        prev_tpr2[par] = tpr2
                        pr_done[0] = e1
                        pr_done[1] = e2
                        ln_state["pr_last"][0] = e1
                        ln_state["pr_last"][1] = e2
                        ev_toks.append(e2)
                        tk = None
                        for cb in range(NB):
                            sl = slice(cb * T + s4 * 512,
                                       cb * T + s4 * 512 + 512)
                            b.op("vector", lambda v, sl=sl:
                                 v.tensor_tensor(
                                     SQF[:, sl], LNT[:, sl],
                                     RB[:, s4 * 512:s4 * 512 + 512],
                                     ALU.mult), wait=[e2])
                            tk = b.op("vector", lambda v, sl=sl:
                                      v.tensor_tensor(
                                          LNT[:, sl], SQF[:, sl],
                                          MRB[:, s4 * 512:s4 * 512 + 512],
                                          ALU.subtract), sem=None, inc=0)
                        out_toks.append(tk)
                    return run
                apply_pending[0] = mk_tail()
            apply_pending[0]()
            return out_toks

        trig_state = {"u_free": [None, None, None], "sins": None,
                      "fs_free": [None, None], "uc": 0, "fc": 0}

        def emit_trig_half(utoks, cdst, sdst, sl, pf=None, ui=0):
            fi = trig_state["fc"] % 2
            trig_state["fc"] += 1
            """u (128,1024 f32, >=0, in 'turns') -> sin into sdst[:, sl],
            cos into cdst[:, sl] (bf16). u is double-buffered (ui) so the
            producer of the next half never waits this half's cos chain.
            Robust to either truncating or round-to-nearest f32->i32
            conversion: after f = u - cvt(u), subtract 1 where f > 0.5 so
            f lands in [-0.5, 0.5]."""
            u = UB2[ui]
            fs = FSB[fi]
            b.op("vector", lambda v, u=u: v.tensor_scalar(
                UF[:], u[:], 12582912.0, 12582912.0,
                ALU.add, ALU.subtract), wait=list(utoks))
            t1 = b.op("vector", lambda v, u=u, fs=fs: v.tensor_sub(
                fs, u[:], UF[:]),
                wait=[trig_state["fs_free"][fi]], sem=None, inc=0)
            ts1 = b.op("scalar", lambda s, fs=fs: s.activation(
                sdst[:, sl], fs, AF.Sin, scale=TPI),
                wait=[t1] + ([pf] if pf else []), sem=None, inc=0)
            tabs = b.op("scalar", lambda s, u=u, fs=fs: s.activation(
                u[:], fs, AF.Abs), sem=None, inc=0)
            trig_state["fs_free"][fi] = tabs
            ts2 = b.op("scalar", lambda s, u=u: s.activation(
                cdst[:, sl], u[:], AF.Sin, scale=-TPI, bias=HPI[:, 0:1]),
                wait=([pf] if pf else []), sem=None, inc=0)
            trig_state["u_free"][ui] = ts2
            trig_state["sins"] = ts2
            return ts2

        def emit_attna(ln_toks):
            wtoks = [None] * (NB + 2)
            free_ws = [None, None]

            def plan_wdma(cb):
                if cb >= NB:
                    return
                ws = WSA if cb % 2 == 0 else WSB
                dsem = "dwa" if cb % 2 == 0 else "dwb"
                wait = [free_ws[cb % 2]] if free_ws[cb % 2] else []
                wslice_dma(ws[:, 0:1024], wk_d, cb * 128, wait, dsem)
                wtoks[cb] = wslice_dma(ws[:, 1024:2048], wv_d, cb * 128, [],
                                       dsem)

            plan_wdma(0)
            plan_wdma(1)
            big_toks = []
            prev_kr_evac = [None, None]
            prev_v_evac = [None, None]
            kr_last = [None] * NB
            trig_theta = [None] * NB
            trig_done = [None] * NB
            vev_done = [None] * NB
            pool_free = [None]
            pool_free_th = [None, None]
            vbf_free = [None]

            def plan_kr_theta(cb):
                ws = WSA if cb % 2 == 0 else WSB
                kr_evac = [None] * 4
                ui_th = (trig_state["uc"] % 3,
                         (trig_state["uc"] + 1) % 3)
                for s4 in range(4):
                    th = s4 // 2
                    def mm_kr(t, s4=s4, ws=ws):
                        for i in range(NB):
                            ins = t.matmul(
                                PC[:, (s4 % 2) * 512:(s4 % 2) * 512 + 512],
                                ws[:, i * 128:i * 128 + 128],
                                LNT[:, i * T + s4 * 512:
                                    i * T + s4 * 512 + 512],
                                start=(i == 0), stop=(i == NB - 1))
                        return ins
                    w_kr = [wtoks[cb], ln_toks[s4]] +                         [kr_evac[s4 - 2] if s4 >= 2
                         else prev_kr_evac[s4 % 2]]
                    tkr = b.op("tensor", mm_kr, wait=w_kr, sem="T", inc=1)
                    ev = b.op("scalar", lambda s, s4=s4, th=th,
                              ui_th=ui_th: s.activation(
                                  UB2[ui_th[th]][:, (s4 % 2) * 512:
                                                 (s4 % 2) * 512 + 512],
                                  PC[:, (s4 % 2) * 512:
                                     (s4 % 2) * 512 + 512],
                                  AF.Identity, scale=PH / TPI,
                                  bias=B512[:, 0:1]),
                              wait=[tkr,
                                    trig_state["u_free"][ui_th[th]]]
                              + list(ln_state["po_reads"]),
                              sem=None, inc=0)
                    kr_evac[s4] = ev
                for th in range(2):
                    emit_trig_half([kr_evac[th * 2], kr_evac[th * 2 + 1]],
                                   CTH2_[cb % 2], STH2_[cb % 2],
                                   slice(th * 1024, th * 1024 + 1024),
                                   pf=pool_free_th[cb % 2], ui=ui_th[th])
                    trig_state["uc"] += 1
                prev_kr_evac[0] = kr_evac[2]
                prev_kr_evac[1] = kr_evac[3]
                kr_last[cb] = kr_evac[3]
                trig_theta[cb] = b.tok("SC")

            def plan_v_phi(cb):
                ws = WSA if cb % 2 == 0 else WSB
                v_evac = [None] * 4
                for s4 in range(4):
                    def mm_v(t, s4=s4, ws=ws):
                        for i in range(NB):
                            ins = t.matmul(
                                PD[:, (s4 % 2) * 512:(s4 % 2) * 512 + 512],
                                ws[:, 1024 + i * 128:1024 + i * 128 + 128],
                                LNT[:, i * T + s4 * 512:i * T + s4 * 512 + 512],
                                start=(i == 0), stop=(i == NB - 1))
                        return ins
                    w_v = [wtoks[cb], ln_toks[s4]] +                         [v_evac[s4 - 2] if s4 >= 2 else prev_v_evac[s4 % 2]]
                    tv = b.op("tensor", mm_v, wait=w_v, sem="T", inc=1)
                    ev = b.op("scalar", lambda s, s4=s4, cb=cb: s.activation(
                        VBF[:, s4 * 512:s4 * 512 + 512],
                        PD[:, (s4 % 2) * 512:(s4 % 2) * 512 + 512],
                        AF.Copy), wait=[tv, vbf_free[0]], sem=None, inc=0)
                    v_evac[s4] = ev
                prev_v_evac[0] = v_evac[2]
                prev_v_evac[1] = v_evac[3]
                vev_done[cb] = v_evac[3]
                for th in range(2):
                    ui = trig_state["uc"] % 3
                    gcol = DER[:, 8 + cb:8 + cb + 1]
                    g1kcol = DER[:, 16 + cb:16 + cb + 1]
                    tphi = b.op("vector",
                                lambda v, gcol=gcol, g1kcol=g1kcol, th=th,
                                ui=ui: v.tensor_scalar(
                                    UB2[ui][:], TROW[:], gcol,
                                    (g1kcol if th == 1 else 0.0),
                                    ALU.mult, ALU.add),
                                wait=[tok_consts, tok_der, kr_last[cb],
                                      trig_state["u_free"][ui]],
                                sem=None, inc=0)
                    emit_trig_half([tphi], CPH, SPH,
                                   slice(th * 1024, th * 1024 + 1024),
                                   pf=pool_free[0], ui=ui)
                    trig_state["uc"] += 1
                trig_done[cb] = b.tok("SC")

            def plan_p2(cb):
                CTH = CTH2_[cb % 2]
                STH = STH2_[cb % 2]
                gpcol = VECS[:, 24 + cb:24 + cb + 1]
                gacol = VECS[:, 32 + cb:32 + cb + 1]
                w0 = [trig_theta[cb], vev_done[cb]]
                b.op("vector", lambda v: v.memset(PRD[:, 0:1], 0.0),
                     wait=w0)
                b.op("vector", lambda v, CTH=CTH: v.tensor_tensor(
                    PRD[:, 1:T], VBF[:, 1:T], CTH[:, 0:T - 1], ALU.mult))
                b.op("vector", lambda v: v.tensor_tensor_scan(
                    SC1[:], PRD[:], PRD[:], 0.0, ALU.add, ALU.bypass))
                b.op("vector", lambda v, CTH=CTH: v.tensor_tensor(
                    SC1[:], SC1[:], CTH[:], ALU.mult))
                b.op("vector", lambda v: v.memset(PRD[:, 0:1], 0.0))
                b.op("vector", lambda v, STH=STH: v.tensor_tensor(
                    PRD[:, 1:T], VBF[:, 1:T], STH[:, 0:T - 1], ALU.mult))
                b.op("vector", lambda v: v.tensor_tensor_scan(
                    SC2[:], PRD[:], PRD[:], 0.0, ALU.add, ALU.bypass))
                tthl = b.op("vector", lambda v, STH=STH: v.tensor_tensor(
                    SC2[:], SC2[:], STH[:], ALU.mult), sem=None, inc=0)
                pool_free_th[cb % 2] = tthl
                b.op("vector", lambda v: v.tensor_tensor(
                    SC1[:], SC1[:], SC2[:], ALU.add))
                b.op("vector", lambda v, gacol=gacol: v.tensor_single_scalar(
                    SC1[:], SC1[:], gacol, ALU.mult))
                b.op("vector", lambda v: v.tensor_tensor(
                    PRD[:], VBF[:], CPH[:], ALU.mult),
                    wait=[trig_done[cb]])
                b.op("vector", lambda v: v.tensor_tensor_scan(
                    SC2[:], PRD[:], PRD[:], 0.0, ALU.add, ALU.bypass))
                b.op("vector", lambda v: v.tensor_tensor(
                    SC2[:], SC2[:], CPH[:], ALU.mult))
                b.op("vector", lambda v: v.tensor_tensor(
                    PRD[:], VBF[:], SPH[:], ALU.mult))
                b.op("vector", lambda v: v.tensor_tensor_scan(
                    VBF[:], PRD[:], PRD[:], 0.0, ALU.add, ALU.bypass))
                tphl = b.op("vector", lambda v: v.tensor_tensor(
                    VBF[:], VBF[:], SPH[:], ALU.mult), sem=None, inc=0)
                pool_free[0] = tphl
                tvf = b.op("vector", lambda v: v.tensor_tensor(
                    SC2[:], SC2[:], VBF[:], ALU.add), sem=None, inc=0)
                vbf_free[0] = tvf
                b.op("vector", lambda v, gpcol=gpcol: v.tensor_single_scalar(
                    SC2[:], SC2[:], gpcol, ALU.mult))
                b.op("vector", lambda v: v.tensor_tensor(
                    SC1[:], SC1[:], SC2[:], ALU.add))
                tk = b.op("vector", lambda v, cb=cb: v.tensor_tensor(
                    BIG[:, cb * T:(cb + 1) * T], SC1[:], ISQ[:], ALU.mult),
                    wait=[tok_consts] + ([sgn_all_l[0]] if cb == 7
                                         else []),
                    sem=None, inc=0)
                big_toks.append(tk)

            plan_kr_theta(0)
            for cb in range(NB):
                if cb + 1 < NB:
                    plan_kr_theta(cb + 1)
                plan_v_phi(cb)
                free_ws[cb % 2] = b.tok("T")
                plan_wdma(cb + 2)
                if cb < 4:
                    plan_sign_chunk(2 * cb)
                    plan_sign_chunk(2 * cb + 1)
                plan_p2(cb)
            ln_state["lnt_free"] = b.tok("T")
            res_toks = emit_proj(big_toks, woa_d, bias_col=None)
            return res_toks

        def emit_proj(src_toks, w_dram, bias_col, out_dma=False):
            free_ws = [None, None]
            wtok = [None] * (NB + 2)

            def plan_w(mb):
                if mb >= NB:
                    return
                ws = WSA if mb % 2 == 0 else WSB
                dsem = "dwa" if mb % 2 == 0 else "dwb"
                wait = ([free_ws[mb % 2]] if free_ws[mb % 2]
                        else [b.tok("T")])
                wtok[mb] = wslice_dma(ws[:, 0:1024], w_dram, mb * 128, wait,
                                      dsem)

            plan_w(0)
            plan_w(1)
            res_toks = []
            prev_ev = [None, None]
            for mb in range(NB):
                ws = WSA if mb % 2 == 0 else WSB
                for s4 in range(4):
                    def mm(t, mb=mb, s4=s4, ws=ws):
                        for i in range(NB):
                            ins = t.matmul(
                                PC[:, (s4 % 2) * 512:(s4 % 2) * 512 + 512],
                                ws[:, i * 128:i * 128 + 128],
                                BIG[:, i * T + s4 * 512:i * T + s4 * 512 + 512],
                                start=(i == 0), stop=(i == NB - 1))
                        return ins
                    w = [wtok[mb]] + src_toks + \
                        ([prev_ev[s4 % 2]] if prev_ev[s4 % 2] else [])
                    tmm = b.op("tensor", mm, wait=w, sem="T", inc=1)
                    if bias_col is not None:
                        def ev(v, mb=mb, s4=s4):
                            return v.scalar_tensor_tensor(
                                XT[:, mb * T + s4 * 512:
                                   mb * T + s4 * 512 + 512],
                                PC[:, (s4 % 2) * 512:(s4 % 2) * 512 + 512],
                                bias_col[:, mb:mb + 1],
                                XT[:, mb * T + s4 * 512:
                                   mb * T + s4 * 512 + 512],
                                ALU.add, ALU.add)
                    else:
                        def ev(v, mb=mb, s4=s4):
                            return v.tensor_tensor(
                                XT[:, mb * T + s4 * 512:
                                   mb * T + s4 * 512 + 512],
                                XT[:, mb * T + s4 * 512:
                                   mb * T + s4 * 512 + 512],
                                PC[:, (s4 % 2) * 512:(s4 % 2) * 512 + 512],
                                ALU.add)
                    tev = b.op("vector", ev, wait=[tmm], sem=None, inc=0)
                    prev_ev[s4 % 2] = tev
                    if s4 == 3:
                        res_toks.append(tev)
                        if out_dma:
                            b.op("sync", lambda s, mb=mb: s.dma_start(
                                out=yT_d.rearrange("(g p) t -> p g t",
                                                   p=128)[:, mb, :],
                                in_=XT[:, mb * T:mb * T + T]),
                                wait=[tev], sem="dy", inc=16)
                free_ws[mb % 2] = b.tok("T")
                plan_w(mb + 2)
            ln_state["lnt_free"] = b.tok("T")
            return res_toks

        def emit_ffn(ln_toks, out_dma=False):
            free_ws = [None, None]
            wtok = [None] * (NB + 2)

            def plan_w(mb):
                if mb >= NB:
                    return
                ws = WSA if mb % 2 == 0 else WSB
                dsem = "dfa" if mb % 2 == 0 else "dfb"
                wait = ([free_ws[mb % 2]] if free_ws[mb % 2]
                        else [b.tok("T")])
                wtok[mb] = b.op("sync", lambda s, ws=ws, mb=mb: s.dma_start(
                    out=ws[:, 0:1024],
                    in_=wsgn_d[:, mb * 1024:mb * 1024 + 1024]),
                    wait=wait + [sgn_tok[mb]], sem=dsem, inc=16,
                    is_dma=True)

            plan_w(0)
            plan_w(1)
            res_toks = []
            prev_ev = [None, None]
            for mb in range(NB):
                ws = WSA if mb % 2 == 0 else WSB
                for s4 in range(4):
                    def mm(t, mb=mb, s4=s4, ws=ws):
                        for i in range(NB):
                            ins = t.matmul(
                                PC[:, (s4 % 2) * 512:(s4 % 2) * 512 + 512],
                                ws[:, i * 128:i * 128 + 128],
                                LNT[:, i * T + s4 * 512:i * T + s4 * 512 + 512],
                                start=(i == 0), stop=(i == NB - 1))
                        return ins
                    w = [wtok[mb], ln_toks[s4]] + \
                        ([prev_ev[s4 % 2]] if prev_ev[s4 % 2] else [])
                    tmm = b.op("tensor", mm, wait=w, sem="T", inc=1)
                    tev = b.op("vector",
                               lambda v, mb=mb, s4=s4: v.tensor_tensor(
                                   XT[:, mb * T + s4 * 512:
                                      mb * T + s4 * 512 + 512],
                                   XT[:, mb * T + s4 * 512:
                                      mb * T + s4 * 512 + 512],
                                   PC[:, (s4 % 2) * 512:(s4 % 2) * 512 + 512],
                                   ALU.add), wait=[tmm], sem=None, inc=0)
                    prev_ev[s4 % 2] = tev
                    if s4 == 3:
                        res_toks.append(tev)
                        if out_dma:
                            b.op("sync", lambda s, mb=mb: s.dma_start(
                                out=yT_d.rearrange("(g p) t -> p g t",
                                                   p=128)[:, mb, :],
                                in_=XT[:, mb * T:mb * T + T]),
                                wait=[tev], sem="dy", inc=16)
                free_ws[mb % 2] = b.tok("T")
                plan_w(mb + 2)
            ln_state["lnt_free"] = b.tok("T")
            return res_toks

        def emit_attnb(ln_toks):
            free_ws = [None, None]
            wtok = [None] * (NB + 2)

            def plan_w(p):
                if p >= NB:
                    return
                ws = WSA if p % 2 == 0 else WSB
                dsem = "dwa" if p % 2 == 0 else "dwb"
                wait = ([free_ws[p % 2]] if free_ws[p % 2]
                        else [b.tok("T")])
                wslice_dma(ws[:, 0:1024], wqT_d, p * 128, wait, dsem)
                wslice_dma(ws[:, 1024:2048], wkbT_d, p * 128, [], dsem)
                wtok[p] = wslice_dma(ws[:, 2048:3072], wvbT_d, p * 128, [],
                                     dsem)

            plan_w(0)
            plan_w(1)
            big_toks = []
            pair_end_T = [None, None]
            qkv_toks = [[], []]
            pr_free = list(ln_state["pr_last"])

            def make_qkv_units(p):
                if p >= NB:
                    return []
                pi = p % 2
                ws = WSA if pi == 0 else WSB
                qt, kt, vsb = QT2[pi], KT2[pi], VSB2[pi]
                wprev = pair_end_T[pi]
                units = []
                toks = qkv_toks[pi]
                toks.clear()

                def u_memset():
                    tk = b.op("vector", lambda v, vsb=vsb: v.memset(
                        vsb[:], 1.0), wait=[wprev], sem=None, inc=0)
                    toks.append(tk)
                units.append(u_memset)
                for s4 in range(4):
                    def u_q(s4=s4):
                        prr = PR if s4 % 2 == 0 else PR2
                        def mm(t, s4=s4, prr=prr):
                            for i in range(NB):
                                ins = t.matmul(
                                    prr[:], ws[:, i * 128:i * 128 + 128],
                                    LNT[:, i * T + s4 * 512:
                                        i * T + s4 * 512 + 512],
                                    start=(i == 0), stop=(i == NB - 1))
                            return ins
                        tq = b.op("tensor", mm,
                                  wait=[wtok[p], pr_free[s4 % 2], wprev,
                                        ln_toks[s4]], sem="T", inc=1)
                        ev = b.op("vector", lambda v, s4=s4, prr=prr:
                                  v.tensor_scalar(
                                      qt[:, s4 * 512:s4 * 512 + 512], prr[:],
                                      0.125, DER[:, p:p + 1],
                                      ALU.mult, ALU.add),
                                  wait=[tq, wprev], sem=None, inc=0)
                        pr_free[s4 % 2] = ev
                        toks.append(ev)
                    units.append(u_q)
                for s4 in range(4):
                    def u_k(s4=s4):
                        prr = PR if s4 % 2 == 0 else PR2
                        def mm(t, s4=s4, prr=prr):
                            for i in range(NB):
                                ins = t.matmul(
                                    prr[:],
                                    ws[:, 1024 + i * 128:1024 + i * 128 + 128],
                                    LNT[:, i * T + s4 * 512:
                                        i * T + s4 * 512 + 512],
                                    start=(i == 0), stop=(i == NB - 1))
                            return ins
                        tk_ = b.op("tensor", mm,
                                   wait=[wtok[p], pr_free[s4 % 2], wprev,
                                         ln_toks[s4]], sem="T", inc=1)
                        ev = b.op("vector", lambda v, s4=s4, prr=prr:
                                  v.tensor_single_scalar(
                                      kt[:, s4 * 512:s4 * 512 + 512], prr[:],
                                      VECS[:, 8 + p:8 + p + 1], ALU.add),
                                  wait=[tk_, wprev], sem=None, inc=0)
                        pr_free[s4 % 2] = ev
                        toks.append(ev)
                    units.append(u_k)
                for tt in range(16):
                    def u_v(tt=tt):
                        prr = PR if tt % 2 == 0 else PR2
                        def mm(t, tt=tt, prr=prr):
                            for i in range(NB):
                                t.matmul(
                                    prr[:, 0:128],
                                    LNT[:, i * T + tt * 128:
                                        i * T + tt * 128 + 128],
                                    ws[:, 2048 + i * 128:2048 + i * 128 + 128],
                                    start=(i == 0), stop=False)
                            return t.matmul(
                                prr[:, 0:128], ONESRB[0:1, :],
                                BVR[0:1, p * 128:p * 128 + 128],
                                start=False, stop=True)
                        tvm = b.op("tensor", mm,
                                   wait=[wtok[p], pr_free[tt % 2], tok_bvr,
                                         toks[0], ln_toks[tt // 4]],
                                   sem="T", inc=1)
                        ev = b.op("vector", lambda v, tt=tt, prr=prr:
                                  v.tensor_copy(
                                      vsb.rearrange(
                                          "p (t h c) -> p t h c", h=2, c=65)
                                      [:, tt, :, 0:64],
                                      prr[:, 0:128].rearrange(
                                          "p (h c) -> p h c", c=64)),
                                  wait=[tvm], sem=None, inc=0)
                        pr_free[tt % 2] = ev
                        toks.append(ev)
                    units.append(u_v)
                return units

            for u in make_qkv_units(0):
                u()
            pending = list(make_qkv_units(1))

            norm_done = None
            pending_norm = [None]
            for p in range(NB):
                pi = p % 2
                QTp, KTp, VSB = QT2[pi], KT2[pi], VSB2[pi]
                qt_all = list(qkv_toks[pi]) + [tok_consts]
                prev_qs_exp = pair_end_T[1 - pi]
                prev_qs_av = pair_end_T[1 - pi]
                for qs in range(4):
                    av_toks = {}
                    exp_toks = []
                    nkt = 4 * qs + 4
                    avq = []

                    def plan_av(kt, qs=qs, nkt=nkt, VSB=VSB,
                                av_toks=av_toks):
                        diag = kt >= 4 * qs
                        qoff = 128 * kt if diag else 512 * qs
                        N = (512 * (qs + 1) - 128 * kt) if diag else 512
                        ep = (EPA, EPB, EPC, EPD)[kt % 4]

                        def mm_av(t, kt=kt, qs=qs, ep=ep, qoff=qoff, N=N,
                                  nkt=nkt, VSB=VSB):
                            o = qoff - 512 * qs
                            t.matmul(PO0[0:65, o:o + N],
                                     VSB[:, kt * 130:kt * 130 + 65],
                                     ep[:, 0:N],
                                     start=(kt == 0), stop=(kt == nkt - 1))
                            return t.matmul(
                                PO1[0:65, o:o + N],
                                VSB[:, kt * 130 + 65:kt * 130 + 130],
                                ep[:, 512:512 + N],
                                start=(kt == 0), stop=(kt == nkt - 1))
                        wav = [exp_toks[kt], qt_all[9 + kt]]
                        if kt == 0 and norm_done is not None:
                            wav.append(norm_done)
                        av_toks[kt] = b.op("tensor", mm_av, wait=wav,
                                           sem="T", inc=1)

                    for kc in range(0, nkt, 2):
                      for kt in (kc, kc + 1):
                        diag = kt >= 4 * qs
                        qoff = 128 * kt if diag else 512 * qs
                        N = (512 * (qs + 1) - 128 * kt) if diag else 512
                        ps = PC if kt % 2 == 0 else PD
                        def mm_s(t, ps=ps, kt=kt, qoff=qoff, N=N, diag=diag,
                                 QTp=QTp, KTp=KTp):
                            t.matmul(ps[:, 0:N],
                                     KTp[0:64, kt * 128:kt * 128 + 128],
                                     QTp[0:64, qoff:qoff + N],
                                     start=True, stop=not diag,
                                     tile_position=(0, 0))
                            ins = t.matmul(
                                ps[:, 512:512 + N],
                                KTp[64:128, kt * 128:kt * 128 + 128],
                                QTp[64:128, qoff:qoff + N],
                                start=True, stop=not diag,
                                tile_position=(64, 0))
                            if diag:
                                t.matmul(ps[:, 0:128], IDENTB[:], MASKB[:],
                                         start=False, stop=True)
                                ins = t.matmul(ps[:, 512:640], IDENTB[:],
                                               MASKB[:], start=False,
                                               stop=True)
                            return ins
                        w = [qt_all[1 + qs], qt_all[5 + kt // 4],
                             tok_consts]
                        if kt >= 2:
                            w.append(exp_toks[kt - 2])
                        elif prev_qs_exp is not None:
                            w.append(prev_qs_exp)
                        ts_ = b.op("tensor", mm_s, wait=w, sem="T", inc=1)
                        exp_toks.append([ts_, None])
                      if pending_norm[0] is not None:
                          norm_done = pending_norm[0]()
                          pending_norm[0] = None
                      if pending:
                          pending.pop(0)()
                      while avq and avq[0] <= kc - 1:
                          plan_av(avq.pop(0))
                      for kt in (kc, kc + 1):
                        diag = kt >= 4 * qs
                        qoff = 128 * kt if diag else 512 * qs
                        N = (512 * (qs + 1) - 128 * kt) if diag else 512
                        ps = PC if kt % 2 == 0 else PD
                        ep = (EPA, EPB, EPC, EPD)[kt % 4]
                        wexp = [exp_toks[kt][0]]
                        if kt >= 4:
                            wexp.append(av_toks[kt - 4])
                        elif prev_qs_av is not None:
                            wexp.append(prev_qs_av)
                        if N == 512:
                            def do_exp(s, ps=ps, ep=ep):
                                return s.activation(
                                    ep[:, 0:1024], ps[:, 0:1024], AF.Exp)
                        else:
                            def do_exp(s, ps=ps, ep=ep, N=N):
                                return s.activation(
                                    ep.rearrange("p (h n) -> p h n", h=2)
                                    [:, :, 0:N],
                                    ps[:].rearrange("p (h n) -> p h n", h=2)
                                    [:, :, 0:N], AF.Exp)
                        te = b.op("scalar", do_exp, wait=wexp, sem=None,
                                  inc=0)
                        exp_toks[kt] = te
                        avq.append(kt)
                    while avq:
                        plan_av(avq.pop(0))
                    prev_qs_exp = exp_toks[-1]
                    prev_qs_av = av_toks[nkt - 1]
                    def make_norm(p=p, qs=qs, av_last=av_toks[nkt - 1]):
                        def run():
                            nts = []
                            for hh, po in ((0, PO0), (1, PO1)):
                                REC = REC2[hh]
                                RECB = RECB2[hh]
                                OSCR = OSCR2[hh]
                                r0 = b.op("scalar", lambda s, po=po, REC=REC:
                                          s.activation(
                                              REC, po[64:65, 0:512],
                                              AF.Ln), wait=[av_last],
                                          sem=None, inc=0)
                                r1 = b.op("scalar", lambda s, REC=REC,
                                          RECB=RECB: s.activation(
                                              RECB, REC, AF.Exp,
                                              scale=-1.0),
                                          wait=[r0], sem=None, inc=0)
                                c1 = b.op("vector", lambda v, po=po,
                                          OSCR=OSCR: v.tensor_copy(
                                              OSCR, po[0:64, 0:512]),
                                          wait=[av_last], sem=None, inc=0)
                                tr1 = b.op("tensor", lambda t, po=po,
                                           RECB=RECB: t.matmul(
                                               po[0:64, 0:512],
                                               ONESRB[0:1, 0:64], RECB,
                                               start=True, stop=True),
                                           wait=[r1, c1], sem="T", inc=1)
                                nt = b.op("vector", lambda v, p=p, qs=qs,
                                          hh=hh, po=po, OSCR=OSCR:
                                          v.tensor_tensor(
                                              BIG[hh * 64:hh * 64 + 64,
                                                  p * T + qs * 512:
                                                  p * T + qs * 512 + 512],
                                              OSCR, po[0:64, 0:512],
                                              ALU.mult),
                                          wait=[tr1], sem=None, inc=0)
                                nts.append(nt)
                                big_toks.append(nt)
                            fin = b.op("vector", lambda v, nts=nts:
                                       v.tensor_copy(REC2[0], REC2[0]),
                                       wait=nts, sem=None, inc=0)
                            return fin
                        return run
                    pending_norm[0] = make_norm()
                while pending:
                    pending.pop(0)()
                if p == NB - 1 and pending_norm[0] is not None:
                    norm_done = pending_norm[0]()
                    pending_norm[0] = None
                pair_end_T[pi] = b.tok("T")
                free_ws[pi] = b.tok("T")
                plan_w(p + 2)
                pending = list(make_qkv_units(p + 2))
            ln_state["lnt_free"] = b.tok("T")
            res = emit_proj(big_toks, wobT_d, bias_col=VECS[:, 16:24])
            return res

        streamed_out = [False]
        done = tok_x_list
        ln_out = None
        for si in range(stop_idx):
            st = STAGES[si]
            if st.startswith("ln"):
                ln_out = emit_ln(done, four_way=(si > 0))
                done = ln_out
            elif st == "attna":
                done = emit_attna(ln_out)
            elif st == "ffn1" or st == "ffn2":
                done = emit_ffn(ln_out, out_dma=(st == "ffn2"))
                if st == "ffn2":
                    streamed_out[0] = True
            elif st == "attnb":
                done = emit_attnb(ln_out)

        final_waits = [b.tok("T"), b.tok("VC"), b.tok("SC"), b.tok("GC")]
        if not streamed_out[0]:
            b.op("sync", lambda s: s.dma_start(
                out=yT_d.rearrange("(g p) t -> p g t", p=128),
                in_=XT[:].rearrange("p (g t) -> p g t", t=T)),
                wait=final_waits, sem="dy", inc=16)
        if stop_after is not None:
            b.op("sync", lambda s: s.dma_start(out=dXT, in_=XT[:]),
                 wait=final_waits, sem="dy", inc=16)
            b.op("gpsimd", lambda g: g.dma_start(out=dLNT, in_=LNT[:]),
                 wait=final_waits, sem="dz", inc=16, is_dma=True)
            b.op("gpsimd", lambda g: g.dma_start(out=dBIG, in_=BIG[:]),
                 wait=final_waits, sem="dz", inc=16, is_dma=True)
            b.op("gpsimd", lambda g: g.dma_start(out=dPOOL, in_=POOL[:]),
                 wait=final_waits, sem="dz", inc=16, is_dma=True)
        b.op("sync", lambda s: s.nop(), wait=[b.tok("dy"), b.tok("dz")])

        block = e(nc.Block())
        b.emit(block, sems)

    return nc


def _prep_inputs(inputs):
    f32 = np.float32
    x = np.asarray(inputs["L0_attn_a_x"])
    wk = np.ascontiguousarray(np.asarray(inputs["L0_attn_a_Wk"], dtype=f32))
    wv = np.ascontiguousarray(np.asarray(inputs["L0_attn_a_Wv"], dtype=f32))
    woa = np.ascontiguousarray(np.asarray(inputs["L0_attn_a_Wo"], dtype=f32))
    gate = np.asarray(inputs["L0_attn_a_gate"], dtype=f32)
    freqs = np.asarray(inputs["L0_attn_a_freqs"], dtype=f32)
    wq = np.asarray(inputs["L0_attn_b_Wq"], dtype=f32)
    bq = np.asarray(inputs["L0_attn_b_bq"], dtype=f32)
    wkb = np.asarray(inputs["L0_attn_b_Wk"], dtype=f32)
    bk = np.asarray(inputs["L0_attn_b_bk"], dtype=f32)
    wvb = np.asarray(inputs["L0_attn_b_Wv"], dtype=f32)
    bv = np.asarray(inputs["L0_attn_b_bv"], dtype=f32)
    wob = np.asarray(inputs["L0_attn_b_Wo"], dtype=f32)
    bo = np.asarray(inputs["L0_attn_b_bo"], dtype=f32)
    wffn = np.asarray(inputs["L0_ffn_weight"], dtype=f32)

    shared = {
        "wk": wk, "wv": wv, "woa": woa,
        "wts": np.ascontiguousarray(wffn.T),
        "wqT": np.ascontiguousarray(wq.T),
        "wkbT": np.ascontiguousarray(wkb.T),
        "wvbT": np.ascontiguousarray(wvb.T),
        "wobT": np.ascontiguousarray(wob.T),
    }
    col = lambda a: np.ascontiguousarray(np.asarray(a).reshape(8, 128).T)
    shared["vecs"] = np.concatenate([
        col(bq), col(bk), col(bo),
        col(np.repeat(gate[:, 0], 64)), col(np.repeat(gate[:, 1], 64)),
        col(freqs.reshape(-1)),
    ], axis=1).astype(f32)
    shared["bvrow"] = np.ascontiguousarray(bv.reshape(1, D))
    shared["trow"] = np.ascontiguousarray(np.broadcast_to(
        np.arange(1024, dtype=f32)[None, :], (128, 1024)))
    isq = 1.0 / np.sqrt(np.arange(1, T + 1, dtype=np.float64))
    shared["isq"] = np.ascontiguousarray(np.broadcast_to(
        isq[None, :], (128, T))).astype(ml_dtypes.bfloat16)
    ki, qi = np.meshgrid(np.arange(128), np.arange(128), indexing="ij")
    shared["identb"] = np.ascontiguousarray(
        np.eye(128, dtype=ml_dtypes.bfloat16))
    shared["maskb"] = np.ascontiguousarray(
        np.where(ki > qi, -1e30, 0.0).astype(ml_dtypes.bfloat16))

    in_maps = []
    for bi in range(8):
        m = dict(shared)
        m["xT"] = np.ascontiguousarray(np.asarray(x[bi], dtype=f32).T)
        in_maps.append(m)
    return in_maps


def kernel(**inputs):
    in_maps = _prep_inputs(inputs)
    nc = build_program()
    res = bass_utils.run_bass_kernel_spmd(nc, in_maps, core_ids=list(range(8)))
    out = np.empty((8, T, D), dtype=np.float32)
    for bi in range(8):
        out[bi] = np.asarray(res.results[bi]["yT"]).T
    return out
